# version 08
# baseline (speedup 1.0000x reference)
import numpy as np
import ml_dtypes
from contextlib import ExitStack
import concourse.bass as bass
import concourse.mybir as mybir
from concourse.bass_utils import run_bass_kernel_spmd

F32 = mybir.dt.float32
BF16 = mybir.dt.bfloat16
U8 = mybir.dt.uint8
AF = mybir.ActivationFunctionType
ALU = mybir.AluOpType
AX = mybir.AxisListType.X

import os as _os
TRSAFE = _os.environ.get("TRSAFE", "0") == "1"
T, D, H, DH = 4096, 1024, 16, 64
NI, IB = 8, 512
NT = 32
DI = 4096
NIC, ICW = 8, 512
NMAC, MW = 4, 1024
KC = 8
C2688 = 448.0 * 6.0
VELT = float(2 ** 20 + 1)
SUBC = float(2 ** 14)
EMIN = float(2 ** -6)


class _Tape:
    def __init__(self):
        self.vals = []
        self.pos = 0
        self.mode = "record"
        self.in_run = False

    def read(self, real):
        if self.in_run:
            return real
        if self.mode == "record":
            self.vals.append(real)
            return real
        v = self.vals[self.pos]
        self.pos += 1
        return v


class Q:
    def __init__(self, nc, name, es, tape):
        self.sem = es.enter_context(nc.semaphore(name))
        self._n = 0
        self._tape = tape

    @property
    def n(self):
        return self._tape.read(self._n)

    def inc(self, inst, k=1):
        inst.then_inc(self.sem, k)
        self._n += k
        return self._n

    def dma(self, inst):
        return self.inc(inst, 16)


class TapeDict(dict):
    def __init__(self, tape):
        super().__init__()
        self._tape = tape

    def get(self, k, default=None):
        return self._tape.read(dict.get(self, k, default))

    def __getitem__(self, k):
        return self._tape.read(dict.__getitem__(self, k))


class EM:
    ENGS = ("gpsimd", "sync", "tensor", "scalar", "vector")

    def __init__(self, block, tape, grouped):
        self.block = block
        self.tape = tape
        self.grouped = grouped
        self.buf = []

    def _run(self, eng, f):
        self.tape.in_run = True
        getattr(self.block, eng)(f)
        self.tape.in_run = False

    def set_phase(self, name):
        sel = _os.environ.get("GRPHASES", "")
        self.cur_grouped = self.grouped and (sel == "" or name in sel.split(","))

    def _emit(self, eng, f):
        if getattr(self, "cur_grouped", self.grouped):
            self.buf.append((eng, f))
        else:
            self._run(eng, f)

    def sync(self, f):
        self._emit("sync", f)

    def gpsimd(self, f):
        self._emit("gpsimd", f)

    def tensor(self, f):
        self._emit("tensor", f)

    def scalar(self, f):
        self._emit("scalar", f)

    def vector(self, f):
        self._emit("vector", f)

    def flush(self):
        for eng in self.ENGS:
            for e, f in self.buf:
                if e == eng:
                    self._run(e, f)
        self.buf = []


def _make_consts():
    cU = np.triu(np.ones((128, 128), np.float32)).astype(ml_dtypes.bfloat16)
    cG = np.zeros((128, 896), np.float32)
    for p in range(128):
        cG[p, p + 384:] = 1.0
    cG = cG.astype(ml_dtypes.bfloat16)
    cId = np.eye(128, dtype=np.float32)
    cSel = np.zeros((8, 512), np.float32)
    for h in range(8):
        cSel[h, 64 * h:64 * h + 64] = 1.0
    cOnR = np.ones((1, 128), np.float32)
    return dict(cU=cU, cG=cG, cId=cId, cSel=cSel, cOnR=cOnR)


def _build_program(tap=None, p01_blocks=NI, p01_qkv=True):
    tape = _Tape()
    if _os.environ.get("NOGROUP", "0") == "1":
        return _build_once(tape, False, tap, p01_blocks, p01_qkv)
    _build_once(tape, False, tap, p01_blocks, p01_qkv)
    tape.mode = "replay"
    nc = _build_once(tape, True, tap, p01_blocks, p01_qkv)
    assert tape.pos == len(tape.vals), (tape.pos, len(tape.vals))
    return nc


def _build_once(tape, grouped, tap=None, p01_blocks=NI, p01_qkv=True):
    K = {"p0": 0, "w": 1, "p01": 2, "wf": 3, "a": 4, "f1": 5, None: 5}[tap]
    nc = bass.Bass("TRN2", target_bir_lowering=False, debug=False)
    dram = lambda n, s, d, k: nc.dram_tensor(n, s, d, kind=k).ap()
    x_h = dram("x", [T, D], F32, "ExternalInput")
    w_h = {k: dram(k, [D, D], F32, "ExternalInput")
           for k in ("wq", "wk", "wv", "wo")}
    gw_h = dram("gw", [DI, D], F32, "ExternalInput")
    uw_h = dram("uw", [DI, D], F32, "ExternalInput")
    dw_h = dram("dw", [D, DI], F32, "ExternalInput")
    sin_h = dram("sin", [1, 1], F32, "ExternalInput")
    sind_h = dram("sind", [1, 1], F32, "ExternalInput")
    cU_h = dram("cU", [128, 128], BF16, "ExternalInput")
    cG_h = dram("cG", [128, 896], BF16, "ExternalInput")
    cId_h = dram("cId", [128, 128], F32, "ExternalInput")
    cSel_h = dram("cSel", [8, 512], F32, "ExternalInput")
    cOnR_h = dram("cOnR", [1, 128], F32, "ExternalInput")
    y_h = dram("y", [T, D], F32, "ExternalOutput")
    sQT = dram("sQT", [KC, 128, T], BF16, "Internal")
    sKT = dram("sKT", [KC, 128, T], BF16, "Internal")
    sXQ = dram("sXQ", [NT, KC, 128, 128], BF16, "Internal")
    sOT = dram("sOT", [KC, 128, T], BF16, "Internal")
    sWG = dram("sWG", [32, KC, 128, 128], BF16, "Internal")
    sWU = dram("sWU", [32, KC, 128, 128], BF16, "Internal")
    sWD = dram("sWD", [8, 32, 128, 128], BF16, "Internal")
    sWO = dram("sWO", [KC, 128, D], BF16, "Internal")

    es = ExitStack()
    sb = lambda n, s, d=F32: es.enter_context(nc.sbuf_tensor(n, s, d))
    qS0 = Q(nc, "qS0", es, tape)
    qS1 = Q(nc, "qS1", es, tape)
    qS2 = Q(nc, "qS2", es, tape)
    qS3 = Q(nc, "qS3", es, tape)
    qS4 = Q(nc, "qS4", es, tape)
    qGD = Q(nc, "qGD", es, tape)
    qTE = Q(nc, "qTE", es, tape)
    qSC = Q(nc, "qSC", es, tape)
    qVE = Q(nc, "qVE", es, tape)
    qBR = Q(nc, "qBR", es, tape)

    cU = sb("bU", [128, 128], BF16)
    cG = sb("bG", [128, 896], BF16)
    cId = sb("bId", [128, 128], F32)
    cSel = sb("bSel", [8, 512], F32)
    cOnR = sb("bOnR", [1, 128], F32)
    cumA = sb("cumA", [65, H], F32)
    cumT = sb("cumT", [65, 1], F32)
    sin6 = sb("sin6", [128, 1], F32)
    rsin = sb("rsin", [128, 1], F32)
    sind6 = sb("sind6", [128, 1], F32)
    rsind = sb("rsind", [128, 1], F32)
    gsW = sb("gsW", [128, 9], F32)
    cSc = sb("cSc", [128, 3], F32)
    epsB = sb("epsB", [128, 1], F32)
    filB = sb("filB", [128, 2], F32)
    esV = ExitStack()
    vAug = esV.enter_context(nc.sbuf_tensor("vAug", [128, NT, H, DH + 1], BF16))

    M = TapeDict(tape)
    bar = [0]
    dbg = {}

    def emit_tap(em, pairs, qsy):
        em.flush()
        marks = [(qTE, qTE.n), (qSC, qSC.n), (qVE, qVE.n), (qGD, qGD.n)]

        def _s(sync):
            for q_, v_ in marks:
                sync.wait_ge(q_.sem, v_)
            for item in pairs:
                if len(item) == 2:
                    name, ap = item
                    t = nc.dram_tensor("dbg_" + name, list(ap.shape), ap.dtype,
                                       kind="ExternalOutput").ap()
                    qsy.dma(sync.dma_start(out=t, in_=ap))
                else:
                    name, dap, shape, dt_ = item
                    t = nc.dram_tensor("dbg_" + name, shape, dt_,
                                       kind="ExternalOutput").ap()
                    qsy.dma(sync.dma_start(out=t, in_=dap))
            sync.wait_ge(qsy.sem, qsy.n)
            sync.nop()
        em.sync(_s)
        em.flush()

    def barrier(em, qsy):
        em.flush()
        bar[0] += 2
        tgt = bar[0]
        sy_n, gd_n = qsy.n, qGD.n
        te_n, sc_n, ve_n = qTE.n, qSC.n, qVE.n

        def _waits(e):
            e.wait_ge(qBR.sem, tgt)
            e.wait_ge(qTE.sem, te_n)
            e.wait_ge(qSC.sem, sc_n)
            e.wait_ge(qVE.sem, ve_n)

        def _s(sync):
            sync.wait_ge(qsy.sem, sy_n)
            sync.nop().then_inc(qBR.sem, 1)
            _waits(sync)

        def _g(g):
            g.wait_ge(qGD.sem, gd_n)
            g.nop().then_inc(qBR.sem, 1)
            _waits(g)

        def _e(e):
            _waits(e)
        em.sync(_s)
        em.gpsimd(_g)
        em.tensor(_e)
        em.scalar(_e)
        em.vector(_e)
        em.flush()

    def emit_quant(v, out_ap, Wd, gs6_ap, rgs_ap, qb):
        nb = Wd // 16
        ax, sg = qb["ax"], qb["sg"]
        am, scb, s8, rdq = qb["am"], qb["sc"], qb["s8"], qb["rdq"]
        a, q2, t1, c_ = qb["a"], qb["q2"], qb["t1"], qb["c_"]
        lo, d_, rdq2 = qb["lo"], qb["d_"], qb["rdq2"]

        def F():
            v.tensor_copy(out=filB[:, 0:1], in_=filB[:, 1:2])
        v.tensor_reduce(out=am[:, 0:nb],
                        in_=ax[:, 0:Wd].rearrange("p (b k) -> p b k", k=16),
                        axis=AX, op=ALU.max)
        F()
        v.tensor_scalar(out=scb[:, 0:nb], in0=am[:, 0:nb], scalar1=gs6_ap,
                        scalar2=448.0, op0=ALU.mult, op1=ALU.min)
        F()
        v.tensor_scalar_mul(out=c_[:, 0:nb], in0=scb[:, 0:nb], scalar1=VELT)
        v.tensor_scalar(out=lo[:, 0:nb], in0=scb[:, 0:nb], scalar1=SUBC,
                        scalar2=SUBC, op0=ALU.add, op1=ALU.subtract)
        v.tensor_scalar(out=qb["msk"][:, 0:nb], in0=scb[:, 0:nb], scalar1=EMIN,
                        scalar2=None, op0=ALU.is_ge)
        v.tensor_sub(out=d_[:, 0:nb], in0=c_[:, 0:nb], in1=scb[:, 0:nb])
        F()
        v.tensor_sub(out=d_[:, 0:nb], in0=c_[:, 0:nb], in1=d_[:, 0:nb])
        F()
        v.select(out=s8[:, 0:nb], mask=qb["msk"][:, 0:nb],
                 on_true=d_[:, 0:nb], on_false=lo[:, 0:nb])
        F()
        v.tensor_scalar_mul(out=rdq[:, 0:nb], in0=s8[:, 0:nb], scalar1=rgs_ap)
        F()
        v.reciprocal(out=rdq2[:, 0:nb], in_=rdq[:, 0:nb])
        v.tensor_scalar_mul(out=s8[:, 0:nb], in0=s8[:, 0:nb], scalar1=0.5)
        v.tensor_tensor(out=a[:, 0:Wd].rearrange("p (b k) -> p b k", k=16),
                        in0=ax[:, 0:Wd].rearrange("p (b k) -> p b k", k=16),
                        in1=rdq2[:, 0:nb].unsqueeze(2).broadcast_to([128, nb, 16]),
                        op=ALU.mult)
        v.tensor_scalar(out=q2[:, 0:Wd], in0=a[:, 0:Wd], scalar1=0.25,
                        scalar2=None, op0=ALU.is_ge)
        for thr in (0.75, 1.25, 1.75):
            v.scalar_tensor_tensor(out=q2[:, 0:Wd], in0=a[:, 0:Wd], scalar=thr,
                                   in1=q2[:, 0:Wd], op0=ALU.is_ge, op1=ALU.add)
        v.tensor_scalar(out=t1[:, 0:Wd], in0=a[:, 0:Wd], scalar1=2.5,
                        scalar2=None, op0=ALU.is_ge)
        v.scalar_tensor_tensor(out=t1[:, 0:Wd], in0=a[:, 0:Wd], scalar=3.5,
                               in1=t1[:, 0:Wd], op0=ALU.is_ge, op1=ALU.add)
        v.scalar_tensor_tensor(out=q2[:, 0:Wd], in0=t1[:, 0:Wd], scalar=2.0,
                               in1=q2[:, 0:Wd], op0=ALU.mult, op1=ALU.add)
        v.tensor_scalar(out=t1[:, 0:Wd], in0=a[:, 0:Wd], scalar1=5.0,
                        scalar2=12.0, op0=ALU.is_ge, op1=ALU.mult)
        v.tensor_max(out=q2[:, 0:Wd], in0=q2[:, 0:Wd], in1=t1[:, 0:Wd])
        v.tensor_mul(out=q2[:, 0:Wd], in0=q2[:, 0:Wd], in1=sg[:, 0:Wd])
        return v.tensor_tensor(out=out_ap.rearrange("p (b k) -> p b k", k=16),
                               in0=q2[:, 0:Wd].rearrange("p (b k) -> p b k", k=16),
                               in1=s8[:, 0:nb].unsqueeze(2)
                               .broadcast_to([128, nb, 16]),
                               op=ALU.mult)

    def quant_bufs(prefix, Wd, stf):
        nb = Wd // 16
        return dict(
            ax=stf(prefix + "ax", [128, Wd], F32),
            sg=stf(prefix + "sg", [128, Wd], F32),
            a=stf(prefix + "a", [128, Wd], F32),
            q2=stf(prefix + "q2", [128, Wd], F32),
            t1=stf(prefix + "t1", [128, Wd], F32),
            am=stf(prefix + "am", [128, nb], F32),
            sc=stf(prefix + "sc", [128, nb], F32),
            s8=stf(prefix + "s8", [128, nb], F32),
            rdq=stf(prefix + "rdq", [128, nb], F32),
            c_=stf(prefix + "c_", [128, nb], F32),
            lo=stf(prefix + "lo", [128, nb], F32),
            d_=stf(prefix + "d_", [128, nb], F32),
            rdq2=stf(prefix + "rdq2", [128, nb], F32),
            msk=stf(prefix + "msk", [128, nb], U8),
        )

    with nc.Block() as block:
        em = EM(block, tape, grouped)
        em.set_phase("p0")
        esC = ExitStack()
        s11 = esC.enter_context(nc.sbuf_tensor("s11", [1, 1], F32))
        s12 = esC.enter_context(nc.sbuf_tensor("s12", [1, 1], F32))
        pB1 = esC.enter_context(nc.psum_tensor("pB1", [128, 1], F32))
        pB2 = esC.enter_context(nc.psum_tensor("pB2", [128, 1], F32))

        def _s(sync):
            for dst, src in ((cU, cU_h), (cG, cG_h), (cId, cId_h),
                             (cSel, cSel_h), (cOnR, cOnR_h)):
                qS0.dma(sync.dma_start(out=dst[:], in_=src))
            qS0.dma(sync.dma_start(out=s11[:], in_=sin_h))
            qS0.dma(sync.dma_start(out=s12[:], in_=sind_h))
        em.sync(_s)
        sy0 = qS0.n

        def _t(t):
            t.wait_ge(qS0.sem, sy0)
            t.matmul(pB1[:], cOnR[:], s11[:], start=True, stop=True)
            qTE.inc(t.matmul(pB2[:], cOnR[:], s12[:], start=True, stop=True))
        em.tensor(_t)
        te0 = qTE.n

        def _v(v):
            v.wait_ge(qTE.sem, te0)
            v.tensor_scalar_mul(out=sin6[:], in0=pB1[:], scalar1=1.0 / 6.0)
            v.reciprocal(out=rsin[:], in_=pB1[:])
            v.tensor_scalar_mul(out=sind6[:], in0=pB2[:], scalar1=1.0 / 6.0)
            v.reciprocal(out=rsind[:], in_=pB2[:])
            v.memset(cumA[:], 0.0)
            v.memset(epsB[:], 1e-5)
            v.memset(filB[:], 0.0)
            qVE.inc(v.memset(vAug[:, :, :, DH:DH + 1], 1.0))
        em.vector(_v)
        em.flush()
        esC.close()

        if K >= 1:
            em.set_phase("w")
            esW = ExitStack()
            esWT = ExitStack()
            sbW = lambda n, s, d=F32: esW.enter_context(nc.sbuf_tensor(n, s, d))
            wT = {k: esWT.enter_context(nc.sbuf_tensor("wT" + k, [128, KC, D], BF16))
                  for k in ("wq", "wk", "wv")}
            wStg = [sbW(f"wStg{i}", [128, D], BF16) for i in range(2)]
            woT = sbW("woT", [128, D], BF16)
            for wi, wname in enumerate(("wq", "wk", "wv", "wo")):
                for r in range(KC):
                    idx = wi * KC + r
                    slot = idx % 2

                    def _g(g, wname=wname, r=r, slot=slot,
                           prev=M.get(("wtr", idx - 2), 0)):
                        g.wait_ge(qS0.sem, prev)
                        qGD.dma(g.dma_start(out=wStg[slot][:],
                                            in_=w_h[wname][128 * r:128 * (r + 1), :]))
                    em.gpsimd(_g)
                    gd_n = qGD.n
                    ve_n = None
                    if wname == "wq":
                        def _v(v, slot=slot, gd_n=gd_n):
                            v.wait_ge(qGD.sem, gd_n)
                            qVE.inc(v.tensor_scalar_mul(out=wStg[slot][:],
                                                        in0=wStg[slot][:],
                                                        scalar1=0.125))
                        em.vector(_v)
                        ve_n = qVE.n

                    def _s(sync, wname=wname, r=r, slot=slot, gd_n=gd_n, ve_n=ve_n):
                        if ve_n is not None:
                            sync.wait_ge(qVE.sem, ve_n)
                        else:
                            sync.wait_ge(qGD.sem, gd_n)
                        if wname != "wo":
                            for kc in range(KC):
                                qS0.dma(sync.dma_start(
                                    out=wT[wname][:, kc, 128 * r:128 * (r + 1)],
                                    in_=wStg[slot][:, 128 * kc:128 * (kc + 1)],
                                    transpose=True))
                        else:
                            for kc in range(KC):
                                qS0.dma(sync.dma_start(
                                    out=woT[:, 128 * kc:128 * (kc + 1)],
                                    in_=wStg[slot][:, 128 * kc:128 * (kc + 1)],
                                    transpose=True))
                            sync.wait_ge(qS0.sem, qS0.n)
                            for kc in range(KC):
                                qS0.dma(sync.dma_start(
                                    out=sWO[kc, :, 128 * r:128 * (r + 1)],
                                    in_=woT[:, 128 * kc:128 * (kc + 1)]))
                    em.sync(_s)
                    M[("wtr", idx)] = qS0.n
            barrier(em, qS0)
            esW.close()

        if K >= 2:
            em.set_phase("p01")
            esP = ExitStack()
            sbP = lambda n, s, d=F32: esP.enter_context(nc.sbuf_tensor(n, s, d))
            ptP = lambda n, s: esP.enter_context(nc.psum_tensor(n, s, F32))
            xin = [sbP(f"xin{i}", [128, D]) for i in range(2)]
            xnf = sbP("xnf", [128, D])
            xnb = [sbP(f"xnb{i}", [128, D], BF16) for i in range(2)]
            xqb = [sbP(f"xqb{i}", [128, D], BF16) for i in range(2)]
            xqTs0 = sbP("xqTs0", [128, KC, 128], BF16)
            xqTs = [xqTs0, xqTs0]
            xnT = sbP("xnT", [128, 2, KC, IB], BF16)
            qbP = quant_bufs("p", D, sbP)
            axP = [qbP["ax"], sbP("pax1", [128, D])]
            sgP = [qbP["sg"], sbP("psg1", [128, D])]
            lnS = [sbP(f"lnS{i}", [128, 8]) for i in range(2)]
            xsq = sbP("xsq", [128, D])
            qkS = [sbP(f"qkS{i}", [128, KC, IB], BF16) for i in range(2)]
            pQK = [ptP(f"pQK{i}", [128, IB]) for i in range(2)]
            pV = [ptP(f"pV{i}", [128, IB]) for i in range(2)]

            for i in range(p01_blocks):
                for tt in range(4):
                    n = 4 * i + tt
                    r = n % 2

                    def _g(g, n=n, r=r, w=M.get(("pq", (4 * i + tt) - 2), 0)):
                        g.wait_ge(qVE.sem, w)
                        qGD.dma(g.dma_start(out=xin[r][:],
                                            in_=x_h[128 * n:128 * (n + 1), :]))
                    em.gpsimd(_g)
                    ld_n = qGD.n

                    def _v(v, r=r, ld_n=ld_n):
                        v.wait_ge(qGD.sem, ld_n)
                        qVE.inc(v.tensor_reduce(out=lnS[r][:, 0:1], in_=xin[r][:],
                                                axis=AX, op=ALU.add, negate=True))
                    em.vector(_v)

                    def _c(c, r=r, ld_n=ld_n, w=M.get(("var", n - 1), 0)):
                        c.wait_ge(qGD.sem, ld_n)
                        c.wait_ge(qVE.sem, w)
                        qSC.inc(c.activation(out=xsq[:], in_=xin[r][:],
                                             func=AF.Square))
                    em.scalar(_c)
                    sc_sq = qSC.n

                    def _v(v, r=r, sc_sq=sc_sq):
                        v.wait_ge(qSC.sem, sc_sq)
                        v.tensor_reduce(out=lnS[r][:, 2:3], in_=xsq[:], axis=AX,
                                        op=ALU.add)
                        v.tensor_scalar_mul(out=lnS[r][:, 1:2], in0=lnS[r][:, 0:1],
                                            scalar1=-1.0 / D)
                        v.tensor_scalar_mul(out=lnS[r][:, 2:3], in0=lnS[r][:, 2:3],
                                            scalar1=1.0 / D)
                        v.tensor_scalar(out=lnS[r][:, 3:4], in0=lnS[r][:, 1:2],
                                        scalar1=lnS[r][:, 1:2], scalar2=None,
                                        op0=ALU.mult)
                        v.tensor_copy(out=filB[:, 0:1], in_=filB[:, 1:2])
                        qVE.inc(v.tensor_sub(out=lnS[r][:, 3:4], in0=lnS[r][:, 2:3],
                                             in1=lnS[r][:, 3:4]))
                    em.vector(_v)
                    ve_var = qVE.n
                    M[("var", n)] = qVE.n

                    def _c(c, r=r, ve_var=ve_var):
                        c.wait_ge(qVE.sem, ve_var)
                        qSC.inc(c.activation(out=lnS[r][:, 4:5], in_=lnS[r][:, 3:4],
                                             func=AF.Sqrt, bias=epsB[:]))
                    em.scalar(_c)
                    sc_std = qSC.n

                    def _v(v, r=r, sc_std=sc_std):
                        v.wait_ge(qSC.sem, sc_std)
                        v.reciprocal(out=lnS[r][:, 6:7], in_=lnS[r][:, 4:5])
                        v.tensor_copy(out=filB[:, 0:1], in_=filB[:, 1:2])
                        v.tensor_copy(out=lnS[r][:, 4:5], in_=lnS[r][:, 6:7])
                        qVE.inc(v.tensor_scalar(out=lnS[r][:, 5:6],
                                                in0=lnS[r][:, 1:2],
                                                scalar1=lnS[r][:, 6:7],
                                                scalar2=-1.0, op0=ALU.mult,
                                                op1=ALU.mult))
                    em.vector(_v)
                    ve_rs = qVE.n

                    def _c(c, r=r, ve_rs=ve_rs, w=M.get(("pq", n - 2), 0),
                           w2=M.get(("xcp", n - 1), 0)):
                        c.wait_ge(qVE.sem, ve_rs)
                        c.wait_ge(qVE.sem, w)
                        c.wait_ge(qVE.sem, w2)
                        c.activation(out=xnf[:], in_=xin[r][:], func=AF.Identity,
                                     bias=lnS[r][:, 5:6], scale=lnS[r][:, 4:5])
                        c.activation(out=axP[r][:], in_=xnf[:], func=AF.Abs)
                        qSC.inc(c.sign(out=sgP[r][:], in_=xnf[:]))
                    em.scalar(_c)
                    sc_xn = qSC.n

                    def _v(v, r=r, sc_xn=sc_xn, w=M.get(("ptr", n - 2), 0),
                           w2=M.get(("qkv", i - 2), 0)):
                        v.wait_ge(qSC.sem, sc_xn)
                        v.wait_ge(qS1.sem, w)
                        v.wait_ge(qTE.sem, w2)
                        qVE.inc(v.tensor_copy(out=xnb[r][:], in_=xnf[:]))
                    em.vector(_v)
                    M[("xcp", n)] = qVE.n

                    def _v(v, r=r):
                        qbP["ax"], qbP["sg"] = axP[r], sgP[r]
                        qVE.inc(emit_quant(v, xqb[r][:], D, sin6[:], rsin[:],
                                           qbP))
                    em.vector(_v)
                    M[("pq", n)] = qVE.n

                    def _s(sync, n=n, r=r, i=i, ve=M[("pq", n)]):
                        sync.wait_ge(qVE.sem, ve)
                        sync.wait_ge(qS1.sem, qS1.n)
                        for kc in range(KC):
                            qS1.dma(sync.dma_start(
                                out=xnT[:, i % 2, kc,
                                        128 * (n % 4):128 * (n % 4 + 1)],
                                in_=xnb[r][:, 128 * kc:128 * (kc + 1)],
                                transpose=True))
                            if TRSAFE:
                                sync.wait_ge(qS1.sem, qS1.n)
                        sync.wait_ge(qS1.sem, qS1.n)
                        for kc in range(KC):
                            qS1.dma(sync.dma_start(
                                out=xqTs[r][:, kc, :],
                                in_=xqb[r][:, 128 * kc:128 * (kc + 1)],
                                transpose=True))
                            if TRSAFE:
                                sync.wait_ge(qS1.sem, qS1.n)
                        sync.wait_ge(qS1.sem, qS1.n)
                        for kc in range(KC):
                            qS1.dma(sync.dma_start(
                                out=sXQ[n, kc], in_=xqTs[r][:, kc, :]))
                    em.sync(_s)
                    M[("ptr", n)] = qS1.n

                if not p01_qkv:
                    continue
                sy_blk = qS1.n
                for wi, wname in enumerate(("wq", "wk")):
                    for oc in range(KC):
                        bk = oc % 2

                        def _t(t, wname=wname, oc=oc, bk=bk, i=i, sy_blk=sy_blk,
                               w=M.get(("pQK", bk), 0)):
                            t.wait_ge(qS1.sem, sy_blk)
                            t.wait_ge(qVE.sem, w)
                            for kc in range(KC):
                                inst = t.matmul(pQK[bk][:],
                                                wT[wname][:, kc, 128 * oc:128 * (oc + 1)],
                                                xnT[:, i % 2, kc, :],
                                                start=(kc == 0), stop=(kc == KC - 1))
                            qTE.inc(inst)
                        em.tensor(_t)
                        te_n = qTE.n

                        def _v(v, wi=wi, oc=oc, bk=bk, te_n=te_n,
                               w=M.get(("qkstore", i - 1), 0)):
                            v.wait_ge(qTE.sem, te_n)
                            v.wait_ge(qS1.sem, w)
                            qVE.inc(v.tensor_copy(out=qkS[wi][:, oc, :],
                                                  in_=pQK[bk][:]))
                        em.vector(_v)
                        M[("pQK", bk)] = qVE.n
                for tt in range(4):
                    for half in range(2):
                        def _t(t, tt=tt, half=half, i=i, sy_blk=sy_blk,
                               w=M.get(("pV", half), 0)):
                            t.wait_ge(qS1.sem, sy_blk)
                            t.wait_ge(qVE.sem, w)
                            for kc in range(KC):
                                inst = t.matmul(pV[half][:],
                                                xnT[:, i % 2, kc, 128 * tt:128 * (tt + 1)],
                                                wT["wv"][:, kc, 512 * half:512 * (half + 1)],
                                                start=(kc == 0), stop=(kc == KC - 1))
                            qTE.inc(inst)
                        em.tensor(_t)
                        te_n = qTE.n

                        def _v(v, tt=tt, half=half, i=i, te_n=te_n):
                            v.wait_ge(qTE.sem, te_n)
                            qVE.inc(v.tensor_copy(
                                out=vAug[:, 4 * i + tt, 8 * half:8 * (half + 1),
                                         0:DH],
                                in_=pV[half][:].rearrange("p (h d) -> p h d", d=DH)))
                        em.vector(_v)
                        M[("pV", half)] = qVE.n
                M[("qkv", i)] = qTE.n
                ve_blk = qVE.n

                def _s(sync, i=i, ve_blk=ve_blk):
                    sync.wait_ge(qVE.sem, ve_blk)
                    for wi, s_t in ((0, sQT), (1, sKT)):
                        for oc in range(KC):
                            qS1.dma(sync.dma_start(
                                out=s_t[oc, :, IB * i:IB * (i + 1)],
                                in_=qkS[wi][:, oc, :]))
                em.sync(_s)
                M[("qkstore", i)] = qS1.n
            if tap == "p01":
                emit_tap(em, [("sQT", sQT, [KC, 128, T], BF16),
                                 ("sKT", sKT, [KC, 128, T], BF16),
                                 ("sXQ", sXQ, [NT, KC, 128, 128], BF16),
                                 ("vAug", vAug[:]),
                                 ("xnb1", xnb[1][:]), ("xqb1", xqb[1][:]),
                                 ("xqTs1", xqTs[1][:]),
                                 ("lnS1", lnS[1][:]),
                                 ("qax", qbP["ax"][:]), ("qa", qbP["a"][:]),
                                 ("qq2", qbP["q2"][:]), ("qam", qbP["am"][:]),
                                 ("qs8", qbP["s8"][:]), ("qrdq", qbP["rdq"][:]),
                                 ("qsg", qbP["sg"][:])], qS1)
            barrier(em, qS1)
            esP.close()
            esWT.close()

        if K >= 3:
            em.set_phase("wf")
            esF = ExitStack()
            sbF = lambda n, s, d=F32: esF.enter_context(nc.sbuf_tensor(n, s, d))
            ptF = lambda n, s: esF.enter_context(nc.psum_tensor(n, s, F32))
            wfi = [sbF(f"wfi{i}", [128, DI]) for i in range(2)]
            qbF = quant_bufs("f", 1024, sbF)
            wqb = sbF("wqb", [128, DI], BF16)
            wTs = sbF("wTs", [128, KC, 128], BF16)
            pT1 = ptF("pT1", [1, 128])
            pG = ptF("pG", [128, 1])
            amx = sbF("amx", [128, 1])
            gm1 = sbF("gm1", [1, 3])
            tdbg = sbF("tdbg", [1, 3 * 128])
            gdbg = sbF("gdbg", [128, 3])
            gdb2 = sbF("gdb2", [128, 6])

            wlist = [("g", gw_h, sWG, 32, 1024, 0), ("u", uw_h, sWU, 32, 1024, 1),
                     ("d", dw_h, sWD, 8, 4096, 2)]
            for tag, src_h, dst_h, ntile, wcols, widx in wlist:
                for r in range(ntile):
                    slot = r % 2

                    def _g(g, src_h=src_h, r=r, slot=slot, wcols=wcols,
                           w=M.get(("wfA", slot), 0)):
                        g.wait_ge(qVE.sem, w)
                        qGD.dma(g.dma_start(
                            out=wfi[slot][:, 0:wcols],
                            in_=src_h[128 * r:128 * (r + 1), :]))
                    em.gpsimd(_g)
                    ld_n = qGD.n

                    def _v(v, widx=widx, r=r, slot=slot, wcols=wcols, ld_n=ld_n):
                        v.wait_ge(qGD.sem, ld_n)
                        v.tensor_reduce(out=amx[:], in_=wfi[slot][:, 0:wcols],
                                        axis=AX, op=ALU.max,
                                        apply_absolute_value=True)
                        v.tensor_copy(out=filB[:, 0:1], in_=filB[:, 1:2])
                        if r == 0:
                            qVE.inc(v.tensor_copy(out=gsW[:, 3 * widx:3 * widx + 1],
                                                  in_=amx[:]))
                        else:
                            qVE.inc(v.tensor_max(out=gsW[:, 3 * widx:3 * widx + 1],
                                                 in0=gsW[:, 3 * widx:3 * widx + 1],
                                                 in1=amx[:]))
                    em.vector(_v)
                    M[("wfA", slot)] = qVE.n
                ve_am = qVE.n

                def _t(t, widx=widx, ve_am=ve_am, w=M.get("pT1", 0)):
                    t.wait_ge(qVE.sem, ve_am)
                    t.wait_ge(qVE.sem, w)
                    qTE.inc(t.transpose(pT1[:], gsW[:, 3 * widx:3 * widx + 1],
                                        cId[:]))
                em.tensor(_t)
                te_tr = qTE.n

                def _v(v, te_tr=te_tr, widx=widx):
                    v.wait_ge(qTE.sem, te_tr)
                    v.tensor_copy(out=tdbg[:, 128 * widx:128 * (widx + 1)],
                                  in_=pT1[:])
                    qVE.inc(v.tensor_reduce(out=gm1[:, widx:widx + 1],
                                            in_=pT1[:], axis=AX, op=ALU.max))
                em.vector(_v)
                M["pT1"] = qVE.n
                ve_gm = qVE.n

                def _t(t, ve_gm=ve_gm, w=M.get("pG", 0), widx=widx):
                    t.wait_ge(qVE.sem, ve_gm)
                    t.wait_ge(qVE.sem, w)
                    qTE.inc(t.matmul(pG[:], cOnR[:], gm1[:, widx:widx + 1],
                                     start=True, stop=True))
                em.tensor(_t)
                te_bc = qTE.n

                def _v(v, widx=widx, te_bc=te_bc):
                    v.wait_ge(qTE.sem, te_bc)
                    v.tensor_copy(out=gdbg[:, widx:widx + 1], in_=pG[:])
                    qVE.inc(v.tensor_scalar_mul(
                        out=gsW[:, 3 * widx + 2:3 * widx + 3],
                        in0=pG[:], scalar1=1.0 / C2688))
                em.vector(_v)
                M["pG"] = qVE.n
            def _v(v):
                F = lambda: v.tensor_copy(out=filB[:, 0:1], in_=filB[:, 1:2])
                for widx in (0, 1, 2):
                    v.reciprocal(out=gsW[:, 3 * widx + 1:3 * widx + 2],
                                 in_=gdbg[:, widx:widx + 1])
                for _ in range(6):
                    F()
                for widx in (0, 1, 2):
                    v.tensor_scalar_mul(
                        out=gsW[:, 3 * widx + 1:3 * widx + 2],
                        in0=gsW[:, 3 * widx + 1:3 * widx + 2],
                        scalar1=C2688 / 6.0)
                for _ in range(2):
                    F()
                v.tensor_scalar_mul(out=cSc[:, 0:1], in0=gsW[:, 2:3],
                                    scalar1=rsin[:])
                v.tensor_scalar_mul(out=cSc[:, 1:2], in0=gsW[:, 5:6],
                                    scalar1=rsin[:])
                v.tensor_scalar_mul(out=cSc[:, 2:3], in0=gsW[:, 8:9],
                                    scalar1=rsind[:])
                qVE.inc(v.tensor_copy(out=gdb2[:, 0:1], in_=gsW[:, 1:2]))
            em.vector(_v)
            veA_end = qVE.n

            for tag, src_h, dst_h, ntile, wcols, widx in wlist:
                gs6_ap = gsW[:, 3 * widx + 1:3 * widx + 2]
                rgs_ap = gsW[:, 3 * widx + 2:3 * widx + 3]
                for r in range(ntile):
                    slot = r % 2

                    def _g(g, src_h=src_h, r=r, slot=slot, wcols=wcols,
                           w=M.get(("wfB", slot), 0), veA_end=veA_end):
                        g.wait_ge(qVE.sem, veA_end)
                        g.wait_ge(qSC.sem, w)
                        qGD.dma(g.dma_start(
                            out=wfi[slot][:, 0:wcols],
                            in_=src_h[128 * r:128 * (r + 1), :]))
                    em.gpsimd(_g)
                    ld_n = qGD.n
                    for sub in range(wcols // 1024):
                        def _c(c, slot=slot, sub=sub, ld_n=ld_n,
                               w=M.get("wfQ", 0)):
                            c.wait_ge(qGD.sem, ld_n)
                            c.wait_ge(qVE.sem, w)
                            c.activation(
                                out=qbF["ax"][:],
                                in_=wfi[slot][:, 1024 * sub:1024 * (sub + 1)],
                                func=AF.Abs)
                            qSC.inc(c.sign(
                                out=qbF["sg"][:],
                                in_=wfi[slot][:, 1024 * sub:1024 * (sub + 1)]))
                        em.scalar(_c)
                        M[("wfB", slot)] = qSC.n
                        sc_n = qSC.n

                        def _v(v, sub=sub, sc_n=sc_n, gs6_ap=gs6_ap,
                               rgs_ap=rgs_ap, w=M.get("wfT", 0)):
                            v.wait_ge(qSC.sem, sc_n)
                            v.wait_ge(qS2.sem, w)
                            qVE.inc(emit_quant(
                                v, wqb[:, 1024 * sub:1024 * (sub + 1)], 1024,
                                gs6_ap, rgs_ap, qbF))
                        em.vector(_v)
                        M["wfQ"] = qVE.n
                    ve_q = qVE.n

                    def _s(sync, dst_h=dst_h, r=r, wcols=wcols, ve_q=ve_q):
                        sync.wait_ge(qVE.sem, ve_q)
                        for c0 in range(0, wcols // 128, KC):
                            sync.wait_ge(qS2.sem, qS2.n)
                            for kc in range(KC):
                                qS2.dma(sync.dma_start(
                                    out=wTs[:, kc, :],
                                    in_=wqb[:, 128 * (c0 + kc):128 * (c0 + kc + 1)],
                                    transpose=True))
                            sync.wait_ge(qS2.sem, qS2.n)
                            for kc in range(KC):
                                if wcols == 1024:
                                    qS2.dma(sync.dma_start(out=dst_h[r, c0 + kc],
                                                           in_=wTs[:, kc, :]))
                                else:
                                    qS2.dma(sync.dma_start(out=dst_h[r, c0 + kc],
                                                           in_=wTs[:, kc, :]))
                    em.sync(_s)
                    M["wfT"] = qS2.n
            if tap == "wf":
                emit_tap(em, [("sWG", sWG, [32, KC, 128, 128], BF16),
                                 ("sWU", sWU, [32, KC, 128, 128], BF16),
                                 ("sWD", sWD, [8, 32, 128, 128], BF16),
                                 ("gsW", gsW[:]), ("gm1", gm1[:]),
                                 ("tdbg", tdbg[:]), ("gdbg", gdbg[:]),
                                 ("gdb2", gdb2[:]),
                                 ("fax", qbF["ax"][:]), ("fsg", qbF["sg"][:]),
                                 ("fam", qbF["am"][:]), ("fs8", qbF["s8"][:]),
                                 ("fa", qbF["a"][:]), ("fq2", qbF["q2"][:]),
                                 ("frdq", qbF["rdq2"][:]), ("fwqb", wqb[:]),
                                 ("fwTs", wTs[:])], qS2)
            barrier(em, qS2)
            esF.close()

        if K >= 4:
            em.set_phase("a")
            esA = ExitStack()
            sbA = lambda n, s, d=F32: esA.enter_context(nc.sbuf_tensor(n, s, d))
            ptA = lambda n, s: esA.enter_context(nc.psum_tensor(n, s, F32))
            ktR = sbA("ktR", [128, KC, T], BF16)
            qtI = [sbA(f"qtI{i}", [128, KC, IB], BF16) for i in range(2)]
            sqb = [sbA(f"sqb{i}", [128, IB], BF16) for i in range(3)]
            nt = sbA("nt", [65, 8, IB])
            den = sbA("den", [8, IB])
            rd = sbA("rd", [8, IB])
            dvT = sbA("dvT", [64, IB], BF16)
            otB = sbA("otB", [128, KC, IB], BF16)
            psA = [ptA(f"psA{i}", [128, IB]) for i in range(2)]
            pnA = [ptA(f"pnA{i}", [65, IB]) for i in range(2)]
            pcc = [ptA(f"pcc{i}", [65, 4]) for i in range(2)]
            prb = [ptA(f"prb{i}", [64, IB]) for i in range(2)]

            def _s(sync):
                for kc in range(KC):
                    qS3.dma(sync.dma_start(out=ktR[:, kc, :], in_=sKT[kc, :, :]))
                for kc in range(KC):
                    qS3.dma(sync.dma_start(out=qtI[0][:, kc, :],
                                           in_=sQT[kc, :, 0:IB]))
            em.sync(_s)
            kt_ld = qS3.n

            for i in range(NI):
                if i + 1 < NI:
                    def _s(sync, i=i, w=M.get(("ablk", i - 1), 0)):
                        sync.wait_ge(qVE.sem, w)
                        for kc in range(KC):
                            qS3.dma(sync.dma_start(
                                out=qtI[(i + 1) % 2][:, kc, :],
                                in_=sQT[kc, :, IB * (i + 1):IB * (i + 2)]))
                    em.sync(_s)
                for g2 in range(2):
                    for hh in range(8):
                        h = 8 * g2 + hh
                        hc, hr = h // 2, (h % 2) * 64
                        jmax = 4 * i + 4
                        numq = {}

                        def emit_num(j):
                            sc_n, ve_m, off = numq.pop(j)

                            def _t(t, h=h, j=j, off=off, sc_n=sc_n, ve_m=ve_m,
                                   w=M.get(("pnA", h % 2), 0)):
                                if ve_m is not None:
                                    t.wait_ge(qVE.sem, ve_m)
                                else:
                                    t.wait_ge(qSC.sem, sc_n)
                                if j == 0:
                                    t.wait_ge(qSC.sem, w)
                                qTE.inc(t.matmul(pnA[h % 2][:, off:IB],
                                                 vAug[:, j, h, :],
                                                 sqb[j % 3][:, off:IB],
                                                 start=(j == 0), stop=False))
                            em.tensor(_t)
                            M[("sqb", j % 3)] = qTE.n

                        for j in range(jmax):
                            m = j - 4 * i
                            bs = j % 2
                            off = 128 * m if m >= 0 else 0

                            def _t(t, i=i, hc=hc, hr=hr, j=j, bs=bs, off=off,
                                   w=M.get(("psA", bs), 0), kt_ld=kt_ld):
                                t.wait_ge(qS3.sem, kt_ld)
                                t.wait_ge(qSC.sem, w)
                                qTE.inc(t.matmul(
                                    psA[bs][:, off:IB],
                                    ktR[hr:hr + 64, hc, 128 * j:128 * (j + 1)],
                                    qtI[i % 2][hr:hr + 64, hc, off:IB],
                                    start=True, stop=True))
                            em.tensor(_t)
                            te_s = qTE.n

                            def _c(c, bs=bs, j=j, off=off, te_s=te_s,
                                   w=M.get(("sqb", j % 3), 0)):
                                c.wait_ge(qTE.sem, te_s)
                                c.wait_ge(qTE.sem, w)
                                qSC.inc(c.activation(out=sqb[j % 3][:, off:IB],
                                                     in_=psA[bs][:, off:IB],
                                                     func=AF.Square, bias=1.0))
                            em.scalar(_c)
                            M[("psA", bs)] = qSC.n
                            sc_n = qSC.n
                            ve_m = None
                            if m >= 0:
                                def _v(v, j=j, m=m, sc_n=sc_n):
                                    v.wait_ge(qSC.sem, sc_n)
                                    qVE.inc(v.tensor_mul(
                                        out=sqb[j % 3][:, 128 * m:128 * (m + 1)],
                                        in0=sqb[j % 3][:, 128 * m:128 * (m + 1)],
                                        in1=cU[:]))
                                em.vector(_v)
                                ve_m = qVE.n
                            numq[j] = (sc_n, ve_m, off)
                            if j >= 1:
                                emit_num(j - 1)
                        emit_num(jmax - 1)

                        def _t(t, i=i, h=h, hh=hh, w=M.get(("pcc", hh % 2), 0)):
                            for m in range(4):
                                t.matmul(pnA[h % 2][:],
                                         vAug[:, 4 * i + m, h, :],
                                         cG[:, 384 - 128 * m:896 - 128 * m],
                                         start=False, stop=(m == 3))
                            t.wait_ge(qVE.sem, w)
                            for m in range(4):
                                inst = t.matmul(pcc[hh % 2][:, m:m + 1],
                                                vAug[:, 4 * i + m, h, :],
                                                cU[:, 127:128], start=True,
                                                stop=True)
                            qTE.inc(inst)
                        em.tensor(_t)
                        te_cum = qTE.n

                        def _c(c, h=h, hh=hh, g2=g2, te_cum=te_cum,
                               w=M.get(("nt", h + 8 if g2 == 0 else h - 8), 0)):
                            c.wait_ge(qTE.sem, te_cum)
                            c.wait_ge(qVE.sem, w)
                            qSC.inc(c.activation(out=nt[:, hh, :], in_=pnA[h % 2][:],
                                                 func=AF.Identity,
                                                 bias=cumA[:, h:h + 1], scale=1.0))
                        em.scalar(_c)
                        M[("pnA", h % 2)] = qSC.n
                        sc_ev = qSC.n

                        def _v(v, h=h, hh=hh, te_cum=te_cum, sc_ev=sc_ev):
                            v.wait_ge(qTE.sem, te_cum)
                            v.wait_ge(qSC.sem, sc_ev)
                            v.tensor_reduce(out=cumT[:], in_=pcc[hh % 2][:],
                                            axis=AX, op=ALU.add)
                            v.tensor_copy(out=filB[:, 0:1], in_=filB[:, 1:2])
                            qVE.inc(v.tensor_add(out=cumA[:, h:h + 1],
                                                 in0=cumA[:, h:h + 1], in1=cumT[:]))
                        em.vector(_v)
                        M[("pcc", hh % 2)] = qVE.n

                        def _g(g, hh=hh, sc_ev=sc_ev):
                            g.wait_ge(qSC.sem, sc_ev)
                            qGD.dma(g.dma_start(out=den[hh:hh + 1, :],
                                                in_=nt[64:65, hh, :]))
                        em.gpsimd(_g)
                    gd_n = qGD.n

                    def _v(v, gd_n=gd_n):
                        v.wait_ge(qGD.sem, gd_n)
                        qVE.inc(v.reciprocal(out=rd[:], in_=den[:]))
                    em.vector(_v)
                    ve_rd = qVE.n
                    for hh in range(8):
                        h = 8 * g2 + hh
                        hc, hr = h // 2, (h % 2) * 64

                        def _t(t, hh=hh, ve_rd=ve_rd, w=M.get(("prb", hh % 2), 0)):
                            t.wait_ge(qVE.sem, ve_rd)
                            t.wait_ge(qVE.sem, w)
                            qTE.inc(t.matmul(prb[hh % 2][:],
                                             cSel[:, 64 * hh:64 * (hh + 1)],
                                             rd[:], start=True, stop=True))
                        em.tensor(_t)
                        te_rb = qTE.n

                        def _v(v, i=i, h=h, hh=hh, hc=hc, hr=hr, te_rb=te_rb,
                               w=M.get(("otst", i - 1), 0), w2=M.get("dvT", 0)):
                            v.wait_ge(qTE.sem, te_rb)
                            v.wait_ge(qS3.sem, w)
                            if hr != 0:
                                v.wait_ge(qGD.sem, w2)
                            dst = otB[0:64, hc, :] if hr == 0 else dvT[:]
                            qVE.inc(v.tensor_tensor(out=dst, in0=nt[0:64, hh, :],
                                                    in1=prb[hh % 2][:],
                                                    op=ALU.mult))
                        em.vector(_v)
                        M[("prb", hh % 2)] = qVE.n
                        M[("nt", h)] = qVE.n
                        ve_dv = qVE.n
                        if hr != 0:
                            def _g(g, hc=hc, ve_dv=ve_dv):
                                g.wait_ge(qVE.sem, ve_dv)
                                qGD.dma(g.dma_start(out=otB[64:128, hc, :],
                                                    in_=dvT[:]))
                            em.gpsimd(_g)
                            M["dvT"] = qGD.n
                M[("ablk", i)] = qVE.n
                ve_blk, gd_blk = qVE.n, qGD.n

                def _s(sync, i=i, ve_blk=ve_blk, gd_blk=gd_blk):
                    sync.wait_ge(qVE.sem, ve_blk)
                    sync.wait_ge(qGD.sem, gd_blk)
                    for kc in range(KC):
                        qS3.dma(sync.dma_start(out=sOT[kc, :, IB * i:IB * (i + 1)],
                                               in_=otB[:, kc, :]))
                em.sync(_s)
                M[("otst", i)] = qS3.n
            barrier(em, qS3)
            esA.close()
            esV.close()

        if K >= 5:
            em.set_phase("f")
            esN = ExitStack()
            sbN = lambda n, s, d=F32: esN.enter_context(nc.sbuf_tensor(n, s, d))
            ptN = lambda n, s: esN.enter_context(nc.psum_tensor(n, s, F32))
            xqTb = sbN("xqTb", [128, KC, MW], BF16)
            gwc = [sbN(f"gwc{i}", [128, KC, ICW], BF16) for i in range(2)]
            uwc = [sbN(f"uwc{i}", [128, KC, ICW], BF16) for i in range(2)]
            dwc = [sbN(f"dwc{i}", [128, 4, D], BF16) for i in range(2)]
            woTb = sbN("woTb", [128, KC, D], BF16)
            accB = sbN("accB", [128, 8, D])
            hs = sbN("hs", [128, ICW])
            hbuf = sbN("hbuf", [128, ICW])
            qbN = quant_bufs("n", ICW, sbN)
            fdrn = sbN("fdrn", [128, 128])
            ax2 = [qbN["ax"], sbN("nax1", [128, ICW])]
            sg2 = [qbN["sg"], sbN("nsg1", [128, ICW])]
            hs2 = [sbN("nhs0", [128, ICW]), sbN("nhs1", [128, ICW])]
            hb2 = [sbN("nhb0", [128, ICW]), sbN("nhb1", [128, ICW])]
            hq = [sbN(f"hq{i}", [128, ICW], BF16) for i in range(2)]
            hqT = [sbN(f"hqT{i}", [128, 4, 128], BF16) for i in range(2)]
            xfin = [sbN(f"xfin{i}", [128, D]) for i in range(2)]
            otT = [sbN(f"otT{i}", [128, KC, 128], BF16) for i in range(2)]
            gbuf = sbN("gbuf", [128, D])
            outb = [sbN(f"outb{i}", [128, D]) for i in range(2)]
            pgu = [ptN(f"pgu{i}", [128, ICW]) for i in range(4)]
            pd0 = ptN("pd0", [128, D])
            pd = [pd0, pd0]
            psS0 = ptN("psS0", [128, D])
            psS = [psS0, psS0]

            def _s(sync):
                for kc in range(KC):
                    qS4.dma(sync.dma_start(out=woTb[:, kc, :], in_=sWO[kc, :, :]))
            em.sync(_s)
            wo_ld = qS4.n

            def _v(v):
                qVE.inc(v.memset(fdrn[:], 0.0))
            em.vector(_v)

            def emit_down(n):
                s = n % 2
                ws = (n // 8) % 2

                def _t(t, s=s, ws=ws, w=M[("hqTsy", n)], w2=M.get(("acc", n - 1), 0)):
                    t.wait_ge(qS4.sem, w)
                    t.wait_ge(qVE.sem, w2)
                    for j in range(4):
                        for nh in range(2):
                            inst = t.matmul(
                                pd0[:, 512 * nh:512 * (nh + 1)],
                                hqT[s][:, j, :],
                                dwc[ws][:, j, 512 * nh:512 * (nh + 1)],
                                start=(j == 0), stop=(j == 3))
                    qTE.inc(inst)
                em.tensor(_t)
                M[("dn", n)] = qTE.n

            def emit_acc(n):
                ic = (n // 8) % NIC
                tt = n % 8

                def _v(v, ic=ic, tt=tt, w=M[("dn", n)],
                       w2=M.get(("finsilu", tt), 0)):
                    v.wait_ge(qTE.sem, w)
                    if ic == 0:
                        v.wait_ge(qSC.sem, w2)
                        v.tensor_scalar_mul(out=accB[:, tt, :],
                                            in0=pd0[:],
                                            scalar1=cSc[:, 2:3])
                    else:
                        v.scalar_tensor_tensor(
                            out=accB[:, tt, :], in0=pd0[:],
                            scalar=cSc[:, 2:3], in1=accB[:, tt, :],
                            op0=ALU.mult, op1=ALU.add)
                    v.tensor_copy(out=fdrn[:, 0:64], in_=fdrn[:, 64:128])
                    qVE.inc(v.tensor_copy(out=fdrn[:, 64:128], in_=fdrn[:, 0:64]))
                em.vector(_v)
                M[("acc", n)] = qVE.n
                if ic == NIC - 1:
                    M[("accdone", tt)] = qVE.n

            def emit_finals(mb):
                for tt in range(8):
                    emit_final_tile(mb, tt)

            def emit_final_tile(mb, tt):
                if True:
                    n = 8 * mb + tt
                    r = tt % 2

                    def _s(sync, n=n, r=r, w=M.get(("fin", r), 0)):
                        sync.wait_ge(qVE.sem, w)
                        qS4.dma(sync.dma_start(out=xfin[r][:],
                                               in_=x_h[128 * n:128 * (n + 1), :]))
                        for kc in range(KC):
                            qS4.dma(sync.dma_start(
                                out=otT[r][:, kc, :],
                                in_=sOT[kc, :, 128 * n:128 * (n + 1)]))
                    em.sync(_s)
                    fin_ld = qS4.n

                    def _c(c, tt=tt, w=M.get("gbuf", 0),
                           w2=M[("accdone", tt)]):
                        c.wait_ge(qVE.sem, w)
                        c.wait_ge(qVE.sem, w2)
                        qSC.inc(c.activation(out=gbuf[:], in_=accB[:, tt, :],
                                             func=AF.Silu))
                    em.scalar(_c)
                    M[("finsilu", tt)] = qSC.n
                    sc_g = qSC.n

                    def _t(t, r=r, fin_ld=fin_ld, w=M.get("psSx", 0),
                           wo_ld=wo_ld):
                        t.wait_ge(qS4.sem, max(fin_ld, wo_ld))
                        t.wait_ge(qVE.sem, w)
                        for kc in range(KC):
                            for nh in range(2):
                                inst = t.matmul(psS[r][:, 512 * nh:512 * (nh + 1)],
                                                otT[r][:, kc, :],
                                                woTb[:, kc, 512 * nh:512 * (nh + 1)],
                                                start=(kc == 0), stop=(kc == KC - 1))
                        qTE.inc(inst)
                    em.tensor(_t)
                    te_s = qTE.n

                    def _v(v, n=n, r=r, te_s=te_s, sc_g=sc_g,
                           w=M.get(("outb", r), 0)):
                        v.wait_ge(qTE.sem, te_s)
                        v.wait_ge(qSC.sem, sc_g)
                        v.wait_ge(qS4.sem, w)
                        v.tensor_tensor(out=gbuf[:], in0=gbuf[:], in1=psS[r][:],
                                        op=ALU.mult)
                        qVE.inc(v.tensor_add(out=outb[r][:], in0=xfin[r][:],
                                             in1=gbuf[:]))
                    em.vector(_v)
                    M["psSx"] = qVE.n
                    M[("fin", r)] = qVE.n
                    M["gbuf"] = qVE.n
                    ve_out = qVE.n

                    def _s(sync, n=n, r=r, ve_out=ve_out):
                        sync.wait_ge(qVE.sem, ve_out)
                        qS4.dma(sync.dma_start(out=y_h[128 * n:128 * (n + 1), :],
                                               in_=outb[r][:]))
                    em.sync(_s)
                    M[("outb", r)] = qS4.n

            _gch = int(_os.environ.get("GRPCHUNK", "4"))
            for n in range(NMAC * NIC * 8):
                mb, ic, tt = n // 64, (n // 8) % NIC, n % 8
                s = n % 2
                gidx = n // 8
                ws = gidx % 2
                if _gch and n % _gch == 0:
                    em.flush()
                if tt == 0:
                    if ic == 0:
                        def _s(sync, mb=mb, w=M.get(("gu", 64 * mb - 1), 0)):
                            sync.wait_ge(qTE.sem, w)
                            for kc in range(KC):
                                for t8 in range(8):
                                    qS4.dma(sync.dma_start(
                                        out=xqTb[:, kc, 128 * t8:128 * (t8 + 1)],
                                        in_=sXQ[8 * mb + t8, kc]))
                        em.sync(_s)
                        M["xqld"] = qS4.n

                    def _s(sync, ic=ic, ws=ws, w=M.get(("dn", 8 * (gidx - 2) + 7), 0)):
                        sync.wait_ge(qTE.sem, w)
                        for kc in range(KC):
                            for j in range(4):
                                qS4.dma(sync.dma_start(
                                    out=gwc[ws][:, kc, 128 * j:128 * (j + 1)],
                                    in_=sWG[4 * (gidx % NIC) + j, kc]))
                                qS4.dma(sync.dma_start(
                                    out=uwc[ws][:, kc, 128 * j:128 * (j + 1)],
                                    in_=sWU[4 * (gidx % NIC) + j, kc]))
                        for j in range(4):
                            for rr in range(8):
                                qS4.dma(sync.dma_start(
                                    out=dwc[ws][:, j, 128 * rr:128 * (rr + 1)],
                                    in_=sWD[rr, 4 * (gidx % NIC) + j]))
                    em.sync(_s)
                    M[("wcld", gidx)] = qS4.n

                def _t(t, tt=tt, s=s, ws=ws, gd_w=max(M["xqld"], M[("wcld", gidx)]),
                       w=M.get(("silu", n - 2), 0), w2=M.get(("hbuf", n - 2), 0)):
                    t.wait_ge(qS4.sem, gd_w)
                    t.wait_ge(qSC.sem, w)
                    t.wait_ge(qVE.sem, w2)
                    for kc in range(KC):
                        t.matmul(pgu[2 * s][:],
                                 xqTb[:, kc, 128 * tt:128 * (tt + 1)],
                                 gwc[ws][:, kc, :],
                                 start=(kc == 0), stop=(kc == KC - 1))
                    for kc in range(KC):
                        inst = t.matmul(pgu[2 * s + 1][:],
                                        xqTb[:, kc, 128 * tt:128 * (tt + 1)],
                                        uwc[ws][:, kc, :],
                                        start=(kc == 0), stop=(kc == KC - 1))
                    qTE.inc(inst)
                em.tensor(_t)
                M[("gu", n)] = qTE.n
                te_gu = qTE.n
                if n >= 1:
                    emit_down(n - 1)

                def _c(c, s=s, te_gu=te_gu, w=M.get(("hbuf", n - 2), 0)):
                    c.wait_ge(qTE.sem, te_gu)
                    c.wait_ge(qVE.sem, w)
                    c.activation(out=hs2[s][:], in_=pgu[2 * s][:],
                                 func=AF.Silu, scale=cSc[:, 0:1])
                    qSC.inc(c.activation(out=fdrn[:, 0:64], in_=fdrn[:, 64:128],
                                         func=AF.Square))
                em.scalar(_c)
                M[("silu", n)] = qSC.n
                sc_si = qSC.n

                def _v(v, s=s, sc_si=sc_si, w=M.get(("absg", n - 2), 0)):
                    v.wait_ge(qSC.sem, sc_si)
                    v.wait_ge(qSC.sem, w)
                    v.scalar_tensor_tensor(
                        out=hb2[s][:], in0=pgu[2 * s + 1][:], scalar=cSc[:, 1:2],
                        in1=hs2[s][:], op0=ALU.mult, op1=ALU.mult)
                    v.tensor_copy(out=fdrn[:, 0:64], in_=fdrn[:, 64:128])
                    qVE.inc(v.tensor_copy(out=fdrn[:, 64:128], in_=fdrn[:, 0:64]))
                em.vector(_v)
                M[("hbuf", n)] = qVE.n
                ve_h = qVE.n

                def _c(c, s=s, ve_h=ve_h, w=M.get(("hquant", n - 2), 0)):
                    c.wait_ge(qVE.sem, ve_h)
                    c.wait_ge(qVE.sem, w)
                    c.activation(out=ax2[s][:], in_=hb2[s][:], func=AF.Abs)
                    c.sign(out=sg2[s][:], in_=hb2[s][:])
                    qSC.inc(c.activation(out=fdrn[:, 0:64], in_=fdrn[:, 64:128],
                                         func=AF.Square))
                em.scalar(_c)
                M[("absg", n)] = qSC.n
                sc_ab = qSC.n

                def _v(v, s=s, sc_ab=sc_ab, w=M.get(("hqTsy", n - 2), 0)):
                    v.wait_ge(qSC.sem, sc_ab)
                    v.wait_ge(qS4.sem, w)
                    qbN["ax"], qbN["sg"] = ax2[s], sg2[s]
                    qVE.inc(emit_quant(v, hq[s][:], ICW, sind6[:],
                                       rsind[:], qbN))
                em.vector(_v)
                M[("hquant", n)] = qVE.n
                ve_hq = qVE.n
                if n >= 1:
                    emit_acc(n - 1)
                if n >= 64 and n % 64 < 8:
                    emit_final_tile(mb - 1, n % 64)

                def _s(sync, s=s, ve_hq=ve_hq, w=M.get(("dn", n - 2), 0)):
                    sync.wait_ge(qVE.sem, ve_hq)
                    sync.wait_ge(qTE.sem, w)
                    sync.wait_ge(qS4.sem, qS4.n)
                    for j in range(4):
                        qS4.dma(sync.dma_start(
                            out=hqT[s][:, j, :],
                            in_=hq[s][:, 128 * j:128 * (j + 1)],
                            transpose=True))
                    sync.wait_ge(qS4.sem, qS4.n)
                em.sync(_s)
                M[("hqTsy", n)] = qS4.n

            NLAST = NMAC * NIC * 8 - 1
            em.flush()
            emit_down(NLAST)
            emit_acc(NLAST)
            em.flush()
            emit_finals(NMAC - 1)
            em.flush()
            fin_sy = qS4.n

            def _s(sync, fin_sy=fin_sy):
                sync.wait_ge(qS4.sem, fin_sy)
                sync.nop()
            em.sync(_s)
            em.flush()
            if tap == "f1":
                emit_tap(em, [("accB", accB[:]), ("gbuf", gbuf[:]),
                                 ("hq", hq[0][:]), ("hs", hs2[0][:]),
                                 ("hbuf", hb2[0][:]), ("xqTb", xqTb[:]),
                                 ("gwc0", gwc[0][:]), ("dwc0", dwc[0][:]),
                                 ("woTb", woTb[:]), ("otT0", otT[0][:]),
                                 ("nax", qbN["ax"][:]), ("nam", qbN["am"][:]),
                                 ("ns8", qbN["s8"][:])], qS4)
            esN.close()
        if K == 0:
            emit_tap(em, [("sin6", sin6[:]), ("rsin", rsin[:]),
                             ("sind6", sind6[:]), ("rsind", rsind[:]),
                             ("cU", cU[:])], qS0)
        elif K == 1:
            emit_tap(em, [("wTq", wT["wq"][:]), ("wTk", wT["wk"][:]),
                             ("wTv", wT["wv"][:]),
                             ("sWO", sWO, [KC, 128, D], BF16)], qS0)
        elif K == 4:
            emit_tap(em, [("sOT", sOT, [KC, 128, T], BF16),
                             ("cumA", cumA[:])], qS3)
        em.flush()
        if K == 1:
            esWT.close()
        if K < 4:
            esV.close()
    es.close()
    return nc


def kernel(L0_attn_x, L0_attn_Wq, L0_attn_Wk, L0_attn_Wv, L0_attn_Wo,
           L0_ffn_gate_w, L0_ffn_up_w, L0_ffn_down_w, L0_ffn_s_in,
           L0_ffn_s_in_down):
    x = np.ascontiguousarray(np.asarray(L0_attn_x, dtype=np.float32))
    B = x.shape[0]
    consts = _make_consts()
    nc = _build_program()
    shared = dict(
        wq=np.ascontiguousarray(L0_attn_Wq, np.float32),
        wk=np.ascontiguousarray(L0_attn_Wk, np.float32),
        wv=np.ascontiguousarray(L0_attn_Wv, np.float32),
        wo=np.ascontiguousarray(L0_attn_Wo, np.float32),
        gw=np.ascontiguousarray(L0_ffn_gate_w, np.float32),
        uw=np.ascontiguousarray(L0_ffn_up_w, np.float32),
        dw=np.ascontiguousarray(L0_ffn_down_w, np.float32),
        sin=np.asarray(L0_ffn_s_in, np.float32).reshape(1, 1),
        sind=np.asarray(L0_ffn_s_in_down, np.float32).reshape(1, 1),
        **consts,
    )
    in_maps = [dict(x=np.ascontiguousarray(x[b]), **shared) for b in range(B)]
    res = run_bass_kernel_spmd(nc, in_maps, core_ids=list(range(B)))
    return np.stack([np.asarray(r["y"], dtype=np.float32)
                     for r in res.results], 0)


# version 10
# speedup vs baseline: 1.1216x; 1.1216x over previous
import numpy as np
import ml_dtypes
from contextlib import ExitStack
import concourse.bass as bass
import concourse.mybir as mybir
from concourse.bass_utils import run_bass_kernel_spmd

F32 = mybir.dt.float32
BF16 = mybir.dt.bfloat16
U8 = mybir.dt.uint8
AF = mybir.ActivationFunctionType
ALU = mybir.AluOpType
AX = mybir.AxisListType.X

import os as _os
TRSAFE = _os.environ.get("TRSAFE", "0") == "1"
T, D, H, DH = 4096, 1024, 16, 64
NI, IB = 8, 512
NT = 32
DI = 4096
NIC, ICW = 8, 512
NMAC, MW = 4, 1024
KC = 8
C2688 = 448.0 * 6.0
VELT = float(2 ** 20 + 1)
SUBC = float(2 ** 14)
EMIN = float(2 ** -6)


class _Tape:
    def __init__(self):
        self.vals = []
        self.pos = 0
        self.mode = "record"
        self.in_run = False

    def read(self, real):
        if self.in_run:
            return real
        if self.mode == "record":
            self.vals.append(real)
            return real
        v = self.vals[self.pos]
        self.pos += 1
        return v


class Q:
    def __init__(self, nc, name, es, tape):
        self.sem = es.enter_context(nc.semaphore(name))
        self._n = 0
        self._tape = tape

    @property
    def n(self):
        return self._tape.read(self._n)

    def inc(self, inst, k=1):
        inst.then_inc(self.sem, k)
        self._n += k
        return self._n

    def dma(self, inst):
        return self.inc(inst, 16)


class TapeDict(dict):
    def __init__(self, tape):
        super().__init__()
        self._tape = tape

    def get(self, k, default=None):
        return self._tape.read(dict.get(self, k, default))

    def __getitem__(self, k):
        return self._tape.read(dict.__getitem__(self, k))


class EM:
    ENGS = ("gpsimd", "sync", "tensor", "scalar", "vector")

    def __init__(self, block, tape, grouped):
        self.block = block
        self.tape = tape
        self.grouped = grouped
        self.buf = []

    def _run(self, eng, f):
        self.tape.in_run = True
        getattr(self.block, eng)(f)
        self.tape.in_run = False

    def set_phase(self, name):
        sel = _os.environ.get("GRPHASES", "")
        self.cur_grouped = self.grouped and (sel == "" or name in sel.split(","))

    def _emit(self, eng, f):
        if getattr(self, "cur_grouped", self.grouped):
            self.buf.append((eng, f))
        else:
            self._run(eng, f)

    def sync(self, f):
        self._emit("sync", f)

    def gpsimd(self, f):
        self._emit("gpsimd", f)

    def tensor(self, f):
        self._emit("tensor", f)

    def scalar(self, f):
        self._emit("scalar", f)

    def vector(self, f):
        self._emit("vector", f)

    def flush(self):
        for eng in self.ENGS:
            for e, f in self.buf:
                if e == eng:
                    self._run(e, f)
        self.buf = []


def _make_consts():
    cU = np.triu(np.ones((128, 128), np.float32)).astype(ml_dtypes.bfloat16)
    cG = np.zeros((128, 896), np.float32)
    for p in range(128):
        cG[p, p + 384:] = 1.0
    cG = cG.astype(ml_dtypes.bfloat16)
    cId = np.eye(128, dtype=np.float32)
    cSel = np.zeros((8, 512), np.float32)
    for h in range(8):
        cSel[h, 64 * h:64 * h + 64] = 1.0
    cOnR = np.ones((1, 128), np.float32)
    return dict(cU=cU, cG=cG, cId=cId, cSel=cSel, cOnR=cOnR)


def _build_program(tap=None, p01_blocks=NI, p01_qkv=True):
    tape = _Tape()
    if _os.environ.get("NOGROUP", "0") == "1":
        return _build_once(tape, False, tap, p01_blocks, p01_qkv)
    _build_once(tape, False, tap, p01_blocks, p01_qkv)
    tape.mode = "replay"
    nc = _build_once(tape, True, tap, p01_blocks, p01_qkv)
    assert tape.pos == len(tape.vals), (tape.pos, len(tape.vals))
    return nc


def _build_once(tape, grouped, tap=None, p01_blocks=NI, p01_qkv=True):
    K = {"p0": 0, "w": 1, "p01": 2, "wf": 3, "a": 4, "f1": 5, None: 5}[tap]
    nc = bass.Bass("TRN2", target_bir_lowering=False, debug=False)
    dram = lambda n, s, d, k: nc.dram_tensor(n, s, d, kind=k).ap()
    x_h = dram("x", [T, D], F32, "ExternalInput")
    w_h = {k: dram(k, [D, D], F32, "ExternalInput")
           for k in ("wq", "wk", "wv", "wo")}
    gw_h = dram("gw", [DI, D], F32, "ExternalInput")
    uw_h = dram("uw", [DI, D], F32, "ExternalInput")
    dw_h = dram("dw", [D, DI], F32, "ExternalInput")
    sin_h = dram("sin", [1, 1], F32, "ExternalInput")
    sind_h = dram("sind", [1, 1], F32, "ExternalInput")
    cU_h = dram("cU", [128, 128], BF16, "ExternalInput")
    cG_h = dram("cG", [128, 896], BF16, "ExternalInput")
    cId_h = dram("cId", [128, 128], F32, "ExternalInput")
    cSel_h = dram("cSel", [8, 512], F32, "ExternalInput")
    cOnR_h = dram("cOnR", [1, 128], F32, "ExternalInput")
    y_h = dram("y", [T, D], F32, "ExternalOutput")
    sQT = dram("sQT", [KC, 128, T], BF16, "Internal")
    sKT = dram("sKT", [KC, 128, T], BF16, "Internal")
    sXQ = dram("sXQ", [NT, KC, 128, 128], BF16, "Internal")
    sOT = dram("sOT", [KC, 128, T], BF16, "Internal")
    sWG = dram("sWG", [32, KC, 128, 128], BF16, "Internal")
    sWU = dram("sWU", [32, KC, 128, 128], BF16, "Internal")
    sWD = dram("sWD", [8, 32, 128, 128], BF16, "Internal")
    sWO = dram("sWO", [KC, 128, D], BF16, "Internal")

    es = ExitStack()
    sb = lambda n, s, d=F32: es.enter_context(nc.sbuf_tensor(n, s, d))
    qS0 = Q(nc, "qS0", es, tape)
    qS1 = Q(nc, "qS1", es, tape)
    qS2 = Q(nc, "qS2", es, tape)
    qS3 = Q(nc, "qS3", es, tape)
    qS4 = Q(nc, "qS4", es, tape)
    qGD = Q(nc, "qGD", es, tape)
    qTE = Q(nc, "qTE", es, tape)
    qSC = Q(nc, "qSC", es, tape)
    qVE = Q(nc, "qVE", es, tape)
    qBR = Q(nc, "qBR", es, tape)

    cU = sb("bU", [128, 128], BF16)
    cG = sb("bG", [128, 896], BF16)
    cId = sb("bId", [128, 128], F32)
    cSel = sb("bSel", [8, 512], F32)
    cOnR = sb("bOnR", [1, 128], F32)
    cumA = sb("cumA", [65, H], F32)
    cumT = sb("cumT", [65, 1], F32)
    sin6 = sb("sin6", [128, 1], F32)
    rsin = sb("rsin", [128, 1], F32)
    sind6 = sb("sind6", [128, 1], F32)
    rsind = sb("rsind", [128, 1], F32)
    gsW = sb("gsW", [128, 9], F32)
    cSc = sb("cSc", [128, 3], F32)
    epsB = sb("epsB", [128, 1], F32)
    filB = sb("filB", [128, 2], F32)
    esV = ExitStack()
    vAug = esV.enter_context(nc.sbuf_tensor("vAug", [128, NT, H, DH + 1], BF16))

    M = TapeDict(tape)
    bar = [0]
    dbg = {}

    def emit_tap(em, pairs, qsy):
        em.flush()
        marks = [(qTE, qTE.n), (qSC, qSC.n), (qVE, qVE.n), (qGD, qGD.n)]

        def _s(sync):
            for q_, v_ in marks:
                sync.wait_ge(q_.sem, v_)
            for item in pairs:
                if len(item) == 2:
                    name, ap = item
                    t = nc.dram_tensor("dbg_" + name, list(ap.shape), ap.dtype,
                                       kind="ExternalOutput").ap()
                    qsy.dma(sync.dma_start(out=t, in_=ap))
                else:
                    name, dap, shape, dt_ = item
                    t = nc.dram_tensor("dbg_" + name, shape, dt_,
                                       kind="ExternalOutput").ap()
                    qsy.dma(sync.dma_start(out=t, in_=dap))
            sync.wait_ge(qsy.sem, qsy.n)
            sync.nop()
        em.sync(_s)
        em.flush()

    def barrier(em, qsy):
        em.flush()
        bar[0] += 2
        tgt = bar[0]
        sy_n, gd_n = qsy.n, qGD.n
        te_n, sc_n, ve_n = qTE.n, qSC.n, qVE.n

        def _waits(e):
            e.wait_ge(qBR.sem, tgt)
            e.wait_ge(qTE.sem, te_n)
            e.wait_ge(qSC.sem, sc_n)
            e.wait_ge(qVE.sem, ve_n)

        def _s(sync):
            sync.wait_ge(qsy.sem, sy_n)
            sync.nop().then_inc(qBR.sem, 1)
            _waits(sync)

        def _g(g):
            g.wait_ge(qGD.sem, gd_n)
            g.nop().then_inc(qBR.sem, 1)
            _waits(g)

        def _e(e):
            _waits(e)
        em.sync(_s)
        em.gpsimd(_g)
        em.tensor(_e)
        em.scalar(_e)
        em.vector(_e)
        em.flush()

    def emit_quant(v, out_ap, Wd, gs6_ap, rgs_ap, qb):
        nb = Wd // 16
        ax, sg = qb["ax"], qb["sg"]
        am, scb, s8, rdq = qb["am"], qb["sc"], qb["s8"], qb["rdq"]
        a, q2, t1, c_ = qb["a"], qb["q2"], qb["t1"], qb["c_"]
        lo, d_, rdq2 = qb["lo"], qb["d_"], qb["rdq2"]

        def F():
            v.tensor_copy(out=filB[:, 0:1], in_=filB[:, 1:2])
        v.tensor_reduce(out=am[:, 0:nb],
                        in_=ax[:, 0:Wd].rearrange("p (b k) -> p b k", k=16),
                        axis=AX, op=ALU.max)
        F()
        v.tensor_scalar(out=scb[:, 0:nb], in0=am[:, 0:nb], scalar1=gs6_ap,
                        scalar2=448.0, op0=ALU.mult, op1=ALU.min)
        F()
        v.tensor_scalar_mul(out=c_[:, 0:nb], in0=scb[:, 0:nb], scalar1=VELT)
        v.tensor_scalar(out=lo[:, 0:nb], in0=scb[:, 0:nb], scalar1=SUBC,
                        scalar2=SUBC, op0=ALU.add, op1=ALU.subtract)
        v.tensor_scalar(out=qb["msk"][:, 0:nb], in0=scb[:, 0:nb], scalar1=EMIN,
                        scalar2=None, op0=ALU.is_ge)
        v.tensor_sub(out=d_[:, 0:nb], in0=c_[:, 0:nb], in1=scb[:, 0:nb])
        F()
        v.tensor_sub(out=d_[:, 0:nb], in0=c_[:, 0:nb], in1=d_[:, 0:nb])
        F()
        v.select(out=s8[:, 0:nb], mask=qb["msk"][:, 0:nb],
                 on_true=d_[:, 0:nb], on_false=lo[:, 0:nb])
        F()
        v.tensor_scalar_mul(out=rdq[:, 0:nb], in0=s8[:, 0:nb], scalar1=rgs_ap)
        F()
        v.reciprocal(out=rdq2[:, 0:nb], in_=rdq[:, 0:nb])
        v.tensor_scalar_mul(out=s8[:, 0:nb], in0=s8[:, 0:nb], scalar1=0.5)
        v.tensor_tensor(out=a[:, 0:Wd].rearrange("p (b k) -> p b k", k=16),
                        in0=ax[:, 0:Wd].rearrange("p (b k) -> p b k", k=16),
                        in1=rdq2[:, 0:nb].unsqueeze(2).broadcast_to([128, nb, 16]),
                        op=ALU.mult)
        v.tensor_scalar(out=q2[:, 0:Wd], in0=a[:, 0:Wd], scalar1=0.25,
                        scalar2=None, op0=ALU.is_ge)
        for thr in (0.75, 1.25, 1.75):
            v.scalar_tensor_tensor(out=q2[:, 0:Wd], in0=a[:, 0:Wd], scalar=thr,
                                   in1=q2[:, 0:Wd], op0=ALU.is_ge, op1=ALU.add)
        v.tensor_scalar(out=t1[:, 0:Wd], in0=a[:, 0:Wd], scalar1=2.5,
                        scalar2=None, op0=ALU.is_ge)
        v.scalar_tensor_tensor(out=t1[:, 0:Wd], in0=a[:, 0:Wd], scalar=3.5,
                               in1=t1[:, 0:Wd], op0=ALU.is_ge, op1=ALU.add)
        v.scalar_tensor_tensor(out=q2[:, 0:Wd], in0=t1[:, 0:Wd], scalar=2.0,
                               in1=q2[:, 0:Wd], op0=ALU.mult, op1=ALU.add)
        v.tensor_scalar(out=t1[:, 0:Wd], in0=a[:, 0:Wd], scalar1=5.0,
                        scalar2=12.0, op0=ALU.is_ge, op1=ALU.mult)
        v.tensor_max(out=q2[:, 0:Wd], in0=q2[:, 0:Wd], in1=t1[:, 0:Wd])
        v.tensor_mul(out=q2[:, 0:Wd], in0=q2[:, 0:Wd], in1=sg[:, 0:Wd])
        return v.tensor_tensor(out=out_ap.rearrange("p (b k) -> p b k", k=16),
                               in0=q2[:, 0:Wd].rearrange("p (b k) -> p b k", k=16),
                               in1=s8[:, 0:nb].unsqueeze(2)
                               .broadcast_to([128, nb, 16]),
                               op=ALU.mult)

    def quant_bufs(prefix, Wd, stf):
        nb = Wd // 16
        return dict(
            ax=stf(prefix + "ax", [128, Wd], F32),
            sg=stf(prefix + "sg", [128, Wd], F32),
            a=stf(prefix + "a", [128, Wd], F32),
            q2=stf(prefix + "q2", [128, Wd], F32),
            t1=stf(prefix + "t1", [128, Wd], F32),
            am=stf(prefix + "am", [128, nb], F32),
            sc=stf(prefix + "sc", [128, nb], F32),
            s8=stf(prefix + "s8", [128, nb], F32),
            rdq=stf(prefix + "rdq", [128, nb], F32),
            c_=stf(prefix + "c_", [128, nb], F32),
            lo=stf(prefix + "lo", [128, nb], F32),
            d_=stf(prefix + "d_", [128, nb], F32),
            rdq2=stf(prefix + "rdq2", [128, nb], F32),
            msk=stf(prefix + "msk", [128, nb], U8),
        )

    with nc.Block() as block:
        em = EM(block, tape, grouped)
        em.set_phase("p0")
        esC = ExitStack()
        s11 = esC.enter_context(nc.sbuf_tensor("s11", [1, 1], F32))
        s12 = esC.enter_context(nc.sbuf_tensor("s12", [1, 1], F32))
        pB1 = esC.enter_context(nc.psum_tensor("pB1", [128, 1], F32))
        pB2 = esC.enter_context(nc.psum_tensor("pB2", [128, 1], F32))

        def _s(sync):
            for dst, src in ((cU, cU_h), (cG, cG_h), (cId, cId_h),
                             (cSel, cSel_h), (cOnR, cOnR_h)):
                qS0.dma(sync.dma_start(out=dst[:], in_=src))
            qS0.dma(sync.dma_start(out=s11[:], in_=sin_h))
            qS0.dma(sync.dma_start(out=s12[:], in_=sind_h))
        em.sync(_s)
        sy0 = qS0.n

        def _t(t):
            t.wait_ge(qS0.sem, sy0)
            t.matmul(pB1[:], cOnR[:], s11[:], start=True, stop=True)
            qTE.inc(t.matmul(pB2[:], cOnR[:], s12[:], start=True, stop=True))
        em.tensor(_t)
        te0 = qTE.n

        def _v(v):
            v.wait_ge(qTE.sem, te0)
            v.tensor_scalar_mul(out=sin6[:], in0=pB1[:], scalar1=1.0 / 6.0)
            v.reciprocal(out=rsin[:], in_=pB1[:])
            v.tensor_scalar_mul(out=sind6[:], in0=pB2[:], scalar1=1.0 / 6.0)
            v.reciprocal(out=rsind[:], in_=pB2[:])
            v.memset(cumA[:], 0.0)
            v.memset(epsB[:], 1e-5)
            v.memset(filB[:], 0.0)
            qVE.inc(v.memset(vAug[:, :, :, DH:DH + 1], 1.0))
        em.vector(_v)
        em.flush()
        esC.close()

        if K >= 1:
            em.set_phase("w")
            esW = ExitStack()
            esWT = ExitStack()
            sbW = lambda n, s, d=F32: esW.enter_context(nc.sbuf_tensor(n, s, d))
            wT = {k: esWT.enter_context(nc.sbuf_tensor("wT" + k, [128, KC, D], BF16))
                  for k in ("wq", "wk", "wv")}
            wStg = [sbW(f"wStg{i}", [128, D], BF16) for i in range(2)]
            woT = sbW("woT", [128, D], BF16)
            for wi, wname in enumerate(("wq", "wk", "wv", "wo")):
                for r in range(KC):
                    idx = wi * KC + r
                    slot = idx % 2

                    def _g(g, wname=wname, r=r, slot=slot,
                           prev=M.get(("wtr", idx - 2), 0)):
                        g.wait_ge(qS0.sem, prev)
                        qGD.dma(g.dma_start(out=wStg[slot][:],
                                            in_=w_h[wname][128 * r:128 * (r + 1), :]))
                    em.gpsimd(_g)
                    gd_n = qGD.n
                    ve_n = None
                    if wname == "wq":
                        def _v(v, slot=slot, gd_n=gd_n):
                            v.wait_ge(qGD.sem, gd_n)
                            qVE.inc(v.tensor_scalar_mul(out=wStg[slot][:],
                                                        in0=wStg[slot][:],
                                                        scalar1=0.125))
                        em.vector(_v)
                        ve_n = qVE.n

                    def _s(sync, wname=wname, r=r, slot=slot, gd_n=gd_n, ve_n=ve_n):
                        if ve_n is not None:
                            sync.wait_ge(qVE.sem, ve_n)
                        else:
                            sync.wait_ge(qGD.sem, gd_n)
                        if wname != "wo":
                            for kc in range(KC):
                                qS0.dma(sync.dma_start(
                                    out=wT[wname][:, kc, 128 * r:128 * (r + 1)],
                                    in_=wStg[slot][:, 128 * kc:128 * (kc + 1)],
                                    transpose=True))
                        else:
                            for kc in range(KC):
                                qS0.dma(sync.dma_start(
                                    out=woT[:, 128 * kc:128 * (kc + 1)],
                                    in_=wStg[slot][:, 128 * kc:128 * (kc + 1)],
                                    transpose=True))
                            sync.wait_ge(qS0.sem, qS0.n)
                            for kc in range(KC):
                                qS0.dma(sync.dma_start(
                                    out=sWO[kc, :, 128 * r:128 * (r + 1)],
                                    in_=woT[:, 128 * kc:128 * (kc + 1)]))
                    em.sync(_s)
                    M[("wtr", idx)] = qS0.n
            barrier(em, qS0)
            esW.close()

        if K >= 2:
            em.set_phase("p01")
            esP = ExitStack()
            sbP = lambda n, s, d=F32: esP.enter_context(nc.sbuf_tensor(n, s, d))
            ptP = lambda n, s: esP.enter_context(nc.psum_tensor(n, s, F32))
            xin = [sbP(f"xin{i}", [128, D]) for i in range(2)]
            xnf = sbP("xnf", [128, D])
            xnb = [sbP(f"xnb{i}", [128, D], BF16) for i in range(2)]
            xqb = [sbP(f"xqb{i}", [128, D], BF16) for i in range(2)]
            xqTs = [sbP(f"xqTs{i}", [128, KC, 128], BF16) for i in range(2)]
            xnT = sbP("xnT", [128, 2, KC, IB], BF16)
            qbP = quant_bufs("p", D, sbP)
            lnS = [sbP(f"lnS{i}", [128, 8]) for i in range(2)]
            xsq = sbP("xsq", [128, D])
            qkS = [sbP(f"qkS{i}", [128, KC, IB], BF16) for i in range(2)]
            pQK = [ptP(f"pQK{i}", [128, IB]) for i in range(2)]
            pV = [ptP(f"pV{i}", [128, IB]) for i in range(2)]

            for i in range(p01_blocks):
                for tt in range(4):
                    n = 4 * i + tt
                    r = n % 2

                    def _g(g, n=n, r=r, w=M.get(("pq", (4 * i + tt) - 2), 0)):
                        g.wait_ge(qVE.sem, w)
                        qGD.dma(g.dma_start(out=xin[r][:],
                                            in_=x_h[128 * n:128 * (n + 1), :]))
                    em.gpsimd(_g)
                    ld_n = qGD.n

                    def _v(v, r=r, ld_n=ld_n):
                        v.wait_ge(qGD.sem, ld_n)
                        qVE.inc(v.tensor_reduce(out=lnS[r][:, 0:1], in_=xin[r][:],
                                                axis=AX, op=ALU.add, negate=True))
                    em.vector(_v)

                    def _c(c, r=r, ld_n=ld_n):
                        c.wait_ge(qGD.sem, ld_n)
                        qSC.inc(c.activation(out=xsq[:], in_=xin[r][:],
                                             func=AF.Square))
                    em.scalar(_c)
                    sc_sq = qSC.n

                    def _v(v, r=r, sc_sq=sc_sq):
                        v.wait_ge(qSC.sem, sc_sq)
                        v.tensor_reduce(out=lnS[r][:, 2:3], in_=xsq[:], axis=AX,
                                        op=ALU.add)
                        v.tensor_scalar_mul(out=lnS[r][:, 1:2], in0=lnS[r][:, 0:1],
                                            scalar1=-1.0 / D)
                        v.tensor_scalar_mul(out=lnS[r][:, 2:3], in0=lnS[r][:, 2:3],
                                            scalar1=1.0 / D)
                        v.tensor_scalar(out=lnS[r][:, 3:4], in0=lnS[r][:, 1:2],
                                        scalar1=lnS[r][:, 1:2], scalar2=None,
                                        op0=ALU.mult)
                        v.tensor_copy(out=filB[:, 0:1], in_=filB[:, 1:2])
                        qVE.inc(v.tensor_sub(out=lnS[r][:, 3:4], in0=lnS[r][:, 2:3],
                                             in1=lnS[r][:, 3:4]))
                    em.vector(_v)
                    ve_var = qVE.n

                    def _c(c, r=r, ve_var=ve_var):
                        c.wait_ge(qVE.sem, ve_var)
                        qSC.inc(c.activation(out=lnS[r][:, 4:5], in_=lnS[r][:, 3:4],
                                             func=AF.Sqrt, bias=epsB[:]))
                    em.scalar(_c)
                    sc_std = qSC.n

                    def _v(v, r=r, sc_std=sc_std):
                        v.wait_ge(qSC.sem, sc_std)
                        v.reciprocal(out=lnS[r][:, 6:7], in_=lnS[r][:, 4:5])
                        v.tensor_copy(out=filB[:, 0:1], in_=filB[:, 1:2])
                        v.tensor_copy(out=lnS[r][:, 4:5], in_=lnS[r][:, 6:7])
                        qVE.inc(v.tensor_scalar(out=lnS[r][:, 5:6],
                                                in0=lnS[r][:, 1:2],
                                                scalar1=lnS[r][:, 6:7],
                                                scalar2=-1.0, op0=ALU.mult,
                                                op1=ALU.mult))
                    em.vector(_v)
                    ve_rs = qVE.n

                    def _c(c, r=r, ve_rs=ve_rs, w=M.get(("pq", n - 1), 0)):
                        c.wait_ge(qVE.sem, ve_rs)
                        c.wait_ge(qVE.sem, w)
                        c.activation(out=xnf[:], in_=xin[r][:], func=AF.Identity,
                                     bias=lnS[r][:, 5:6], scale=lnS[r][:, 4:5])
                        c.activation(out=qbP["ax"][:], in_=xnf[:], func=AF.Abs)
                        qSC.inc(c.sign(out=qbP["sg"][:], in_=xnf[:]))
                    em.scalar(_c)
                    sc_xn = qSC.n

                    def _v(v, r=r, sc_xn=sc_xn, w=M.get(("ptr", n - 2), 0),
                           w2=M.get(("qkv", i - 2), 0)):
                        v.wait_ge(qSC.sem, sc_xn)
                        v.wait_ge(qS1.sem, w)
                        v.wait_ge(qTE.sem, w2)
                        v.tensor_copy(out=xnb[r][:], in_=xnf[:])
                        qVE.inc(emit_quant(v, xqb[r][:], D, sin6[:], rsin[:],
                                           qbP))
                    em.vector(_v)
                    M[("pq", n)] = qVE.n

                    def _s(sync, n=n, r=r, i=i, ve=M[("pq", n)]):
                        sync.wait_ge(qVE.sem, ve)
                        sync.wait_ge(qS1.sem, qS1.n)
                        for kc in range(KC):
                            qS1.dma(sync.dma_start(
                                out=xnT[:, i % 2, kc,
                                        128 * (n % 4):128 * (n % 4 + 1)],
                                in_=xnb[r][:, 128 * kc:128 * (kc + 1)],
                                transpose=True))
                            if TRSAFE:
                                sync.wait_ge(qS1.sem, qS1.n)
                        sync.wait_ge(qS1.sem, qS1.n)
                        for kc in range(KC):
                            qS1.dma(sync.dma_start(
                                out=xqTs[r][:, kc, :],
                                in_=xqb[r][:, 128 * kc:128 * (kc + 1)],
                                transpose=True))
                            if TRSAFE:
                                sync.wait_ge(qS1.sem, qS1.n)
                        sync.wait_ge(qS1.sem, qS1.n)
                        for kc in range(KC):
                            qS1.dma(sync.dma_start(
                                out=sXQ[n, kc], in_=xqTs[r][:, kc, :]))
                    em.sync(_s)
                    M[("ptr", n)] = qS1.n

                if not p01_qkv:
                    continue
                sy_blk = qS1.n
                for wi, wname in enumerate(("wq", "wk")):
                    for oc in range(KC):
                        bk = oc % 2

                        def _t(t, wname=wname, oc=oc, bk=bk, i=i, sy_blk=sy_blk,
                               w=M.get(("pQK", bk), 0)):
                            t.wait_ge(qS1.sem, sy_blk)
                            t.wait_ge(qVE.sem, w)
                            for kc in range(KC):
                                inst = t.matmul(pQK[bk][:],
                                                wT[wname][:, kc, 128 * oc:128 * (oc + 1)],
                                                xnT[:, i % 2, kc, :],
                                                start=(kc == 0), stop=(kc == KC - 1))
                            qTE.inc(inst)
                        em.tensor(_t)
                        te_n = qTE.n

                        def _v(v, wi=wi, oc=oc, bk=bk, te_n=te_n,
                               w=M.get(("qkstore", i - 1), 0)):
                            v.wait_ge(qTE.sem, te_n)
                            v.wait_ge(qS1.sem, w)
                            qVE.inc(v.tensor_copy(out=qkS[wi][:, oc, :],
                                                  in_=pQK[bk][:]))
                        em.vector(_v)
                        M[("pQK", bk)] = qVE.n
                for tt in range(4):
                    for half in range(2):
                        def _t(t, tt=tt, half=half, i=i, sy_blk=sy_blk,
                               w=M.get(("pV", half), 0)):
                            t.wait_ge(qS1.sem, sy_blk)
                            t.wait_ge(qVE.sem, w)
                            for kc in range(KC):
                                inst = t.matmul(pV[half][:],
                                                xnT[:, i % 2, kc, 128 * tt:128 * (tt + 1)],
                                                wT["wv"][:, kc, 512 * half:512 * (half + 1)],
                                                start=(kc == 0), stop=(kc == KC - 1))
                            qTE.inc(inst)
                        em.tensor(_t)
                        te_n = qTE.n

                        def _v(v, tt=tt, half=half, i=i, te_n=te_n):
                            v.wait_ge(qTE.sem, te_n)
                            qVE.inc(v.tensor_copy(
                                out=vAug[:, 4 * i + tt, 8 * half:8 * (half + 1),
                                         0:DH],
                                in_=pV[half][:].rearrange("p (h d) -> p h d", d=DH)))
                        em.vector(_v)
                        M[("pV", half)] = qVE.n
                M[("qkv", i)] = qTE.n
                ve_blk = qVE.n

                def _s(sync, i=i, ve_blk=ve_blk):
                    sync.wait_ge(qVE.sem, ve_blk)
                    for wi, s_t in ((0, sQT), (1, sKT)):
                        for oc in range(KC):
                            qS1.dma(sync.dma_start(
                                out=s_t[oc, :, IB * i:IB * (i + 1)],
                                in_=qkS[wi][:, oc, :]))
                em.sync(_s)
                M[("qkstore", i)] = qS1.n
            if tap == "p01":
                emit_tap(em, [("sQT", sQT, [KC, 128, T], BF16),
                                 ("sKT", sKT, [KC, 128, T], BF16),
                                 ("sXQ", sXQ, [NT, KC, 128, 128], BF16),
                                 ("vAug", vAug[:]),
                                 ("xnb1", xnb[1][:]), ("xqb1", xqb[1][:]),
                                 ("xqTs1", xqTs[1][:]),
                                 ("lnS1", lnS[1][:]),
                                 ("qax", qbP["ax"][:]), ("qa", qbP["a"][:]),
                                 ("qq2", qbP["q2"][:]), ("qam", qbP["am"][:]),
                                 ("qs8", qbP["s8"][:]), ("qrdq", qbP["rdq"][:]),
                                 ("qsg", qbP["sg"][:])], qS1)
            barrier(em, qS1)
            esP.close()
            esWT.close()

        if K >= 3:
            em.set_phase("wf")
            esF = ExitStack()
            sbF = lambda n, s, d=F32: esF.enter_context(nc.sbuf_tensor(n, s, d))
            ptF = lambda n, s: esF.enter_context(nc.psum_tensor(n, s, F32))
            wfi = [sbF(f"wfi{i}", [128, DI]) for i in range(2)]
            qbF = quant_bufs("f", 1024, sbF)
            wqb = sbF("wqb", [128, DI], BF16)
            wTs = sbF("wTs", [128, KC, 128], BF16)
            pT1 = ptF("pT1", [1, 128])
            pG = ptF("pG", [128, 1])
            amx = sbF("amx", [128, 1])
            gm1 = sbF("gm1", [1, 3])
            tdbg = sbF("tdbg", [1, 3 * 128])
            gdbg = sbF("gdbg", [128, 3])
            gdb2 = sbF("gdb2", [128, 6])

            wlist = [("g", gw_h, sWG, 32, 1024, 0), ("u", uw_h, sWU, 32, 1024, 1),
                     ("d", dw_h, sWD, 8, 4096, 2)]
            for tag, src_h, dst_h, ntile, wcols, widx in wlist:
                for r in range(ntile):
                    slot = r % 2

                    def _g(g, src_h=src_h, r=r, slot=slot, wcols=wcols,
                           w=M.get(("wfA", slot), 0)):
                        g.wait_ge(qVE.sem, w)
                        qGD.dma(g.dma_start(
                            out=wfi[slot][:, 0:wcols],
                            in_=src_h[128 * r:128 * (r + 1), :]))
                    em.gpsimd(_g)
                    ld_n = qGD.n

                    def _v(v, widx=widx, r=r, slot=slot, wcols=wcols, ld_n=ld_n):
                        v.wait_ge(qGD.sem, ld_n)
                        v.tensor_reduce(out=amx[:], in_=wfi[slot][:, 0:wcols],
                                        axis=AX, op=ALU.max,
                                        apply_absolute_value=True)
                        v.tensor_copy(out=filB[:, 0:1], in_=filB[:, 1:2])
                        if r == 0:
                            qVE.inc(v.tensor_copy(out=gsW[:, 3 * widx:3 * widx + 1],
                                                  in_=amx[:]))
                        else:
                            qVE.inc(v.tensor_max(out=gsW[:, 3 * widx:3 * widx + 1],
                                                 in0=gsW[:, 3 * widx:3 * widx + 1],
                                                 in1=amx[:]))
                    em.vector(_v)
                    M[("wfA", slot)] = qVE.n
                ve_am = qVE.n

                def _t(t, widx=widx, ve_am=ve_am, w=M.get("pT1", 0)):
                    t.wait_ge(qVE.sem, ve_am)
                    t.wait_ge(qVE.sem, w)
                    qTE.inc(t.transpose(pT1[:], gsW[:, 3 * widx:3 * widx + 1],
                                        cId[:]))
                em.tensor(_t)
                te_tr = qTE.n

                def _v(v, te_tr=te_tr, widx=widx):
                    v.wait_ge(qTE.sem, te_tr)
                    v.tensor_copy(out=tdbg[:, 128 * widx:128 * (widx + 1)],
                                  in_=pT1[:])
                    qVE.inc(v.tensor_reduce(out=gm1[:, widx:widx + 1],
                                            in_=pT1[:], axis=AX, op=ALU.max))
                em.vector(_v)
                M["pT1"] = qVE.n
                ve_gm = qVE.n

                def _t(t, ve_gm=ve_gm, w=M.get("pG", 0), widx=widx):
                    t.wait_ge(qVE.sem, ve_gm)
                    t.wait_ge(qVE.sem, w)
                    qTE.inc(t.matmul(pG[:], cOnR[:], gm1[:, widx:widx + 1],
                                     start=True, stop=True))
                em.tensor(_t)
                te_bc = qTE.n

                def _v(v, widx=widx, te_bc=te_bc):
                    v.wait_ge(qTE.sem, te_bc)
                    v.tensor_copy(out=gdbg[:, widx:widx + 1], in_=pG[:])
                    qVE.inc(v.tensor_scalar_mul(
                        out=gsW[:, 3 * widx + 2:3 * widx + 3],
                        in0=pG[:], scalar1=1.0 / C2688))
                em.vector(_v)
                M["pG"] = qVE.n
            def _v(v):
                F = lambda: v.tensor_copy(out=filB[:, 0:1], in_=filB[:, 1:2])
                for widx in (0, 1, 2):
                    v.reciprocal(out=gsW[:, 3 * widx + 1:3 * widx + 2],
                                 in_=gdbg[:, widx:widx + 1])
                for _ in range(6):
                    F()
                for widx in (0, 1, 2):
                    v.tensor_scalar_mul(
                        out=gsW[:, 3 * widx + 1:3 * widx + 2],
                        in0=gsW[:, 3 * widx + 1:3 * widx + 2],
                        scalar1=C2688 / 6.0)
                for _ in range(2):
                    F()
                v.tensor_scalar_mul(out=cSc[:, 0:1], in0=gsW[:, 2:3],
                                    scalar1=rsin[:])
                v.tensor_scalar_mul(out=cSc[:, 1:2], in0=gsW[:, 5:6],
                                    scalar1=rsin[:])
                v.tensor_scalar_mul(out=cSc[:, 2:3], in0=gsW[:, 8:9],
                                    scalar1=rsind[:])
                qVE.inc(v.tensor_copy(out=gdb2[:, 0:1], in_=gsW[:, 1:2]))
            em.vector(_v)
            veA_end = qVE.n

            for tag, src_h, dst_h, ntile, wcols, widx in wlist:
                gs6_ap = gsW[:, 3 * widx + 1:3 * widx + 2]
                rgs_ap = gsW[:, 3 * widx + 2:3 * widx + 3]
                for r in range(ntile):
                    slot = r % 2

                    def _g(g, src_h=src_h, r=r, slot=slot, wcols=wcols,
                           w=M.get(("wfB", slot), 0), veA_end=veA_end):
                        g.wait_ge(qVE.sem, veA_end)
                        g.wait_ge(qSC.sem, w)
                        qGD.dma(g.dma_start(
                            out=wfi[slot][:, 0:wcols],
                            in_=src_h[128 * r:128 * (r + 1), :]))
                    em.gpsimd(_g)
                    ld_n = qGD.n
                    for sub in range(wcols // 1024):
                        def _c(c, slot=slot, sub=sub, ld_n=ld_n,
                               w=M.get("wfQ", 0)):
                            c.wait_ge(qGD.sem, ld_n)
                            c.wait_ge(qVE.sem, w)
                            c.activation(
                                out=qbF["ax"][:],
                                in_=wfi[slot][:, 1024 * sub:1024 * (sub + 1)],
                                func=AF.Abs)
                            qSC.inc(c.sign(
                                out=qbF["sg"][:],
                                in_=wfi[slot][:, 1024 * sub:1024 * (sub + 1)]))
                        em.scalar(_c)
                        M[("wfB", slot)] = qSC.n
                        sc_n = qSC.n

                        def _v(v, sub=sub, sc_n=sc_n, gs6_ap=gs6_ap,
                               rgs_ap=rgs_ap, w=M.get("wfT", 0)):
                            v.wait_ge(qSC.sem, sc_n)
                            v.wait_ge(qS2.sem, w)
                            qVE.inc(emit_quant(
                                v, wqb[:, 1024 * sub:1024 * (sub + 1)], 1024,
                                gs6_ap, rgs_ap, qbF))
                        em.vector(_v)
                        M["wfQ"] = qVE.n
                    ve_q = qVE.n

                    def _s(sync, dst_h=dst_h, r=r, wcols=wcols, ve_q=ve_q):
                        sync.wait_ge(qVE.sem, ve_q)
                        for c0 in range(0, wcols // 128, KC):
                            sync.wait_ge(qS2.sem, qS2.n)
                            for kc in range(KC):
                                qS2.dma(sync.dma_start(
                                    out=wTs[:, kc, :],
                                    in_=wqb[:, 128 * (c0 + kc):128 * (c0 + kc + 1)],
                                    transpose=True))
                            sync.wait_ge(qS2.sem, qS2.n)
                            for kc in range(KC):
                                if wcols == 1024:
                                    qS2.dma(sync.dma_start(out=dst_h[r, c0 + kc],
                                                           in_=wTs[:, kc, :]))
                                else:
                                    qS2.dma(sync.dma_start(out=dst_h[r, c0 + kc],
                                                           in_=wTs[:, kc, :]))
                    em.sync(_s)
                    M["wfT"] = qS2.n
            if tap == "wf":
                emit_tap(em, [("sWG", sWG, [32, KC, 128, 128], BF16),
                                 ("sWU", sWU, [32, KC, 128, 128], BF16),
                                 ("sWD", sWD, [8, 32, 128, 128], BF16),
                                 ("gsW", gsW[:]), ("gm1", gm1[:]),
                                 ("tdbg", tdbg[:]), ("gdbg", gdbg[:]),
                                 ("gdb2", gdb2[:]),
                                 ("fax", qbF["ax"][:]), ("fsg", qbF["sg"][:]),
                                 ("fam", qbF["am"][:]), ("fs8", qbF["s8"][:]),
                                 ("fa", qbF["a"][:]), ("fq2", qbF["q2"][:]),
                                 ("frdq", qbF["rdq2"][:]), ("fwqb", wqb[:]),
                                 ("fwTs", wTs[:])], qS2)
            barrier(em, qS2)
            esF.close()

        if K >= 4:
            em.set_phase("a")
            esA = ExitStack()
            sbA = lambda n, s, d=F32: esA.enter_context(nc.sbuf_tensor(n, s, d))
            ptA = lambda n, s: esA.enter_context(nc.psum_tensor(n, s, F32))
            ktR = sbA("ktR", [128, KC, T], BF16)
            qtI = [sbA(f"qtI{i}", [128, KC, IB], BF16) for i in range(2)]
            sqb = [sbA(f"sqb{i}", [128, IB], BF16) for i in range(3)]
            nt = sbA("nt", [65, 8, IB])
            den = sbA("den", [8, IB])
            rd = sbA("rd", [8, IB])
            dvT = sbA("dvT", [64, IB], BF16)
            otB = sbA("otB", [128, KC, IB], BF16)
            psA = [ptA(f"psA{i}", [128, IB]) for i in range(2)]
            pnA = [ptA(f"pnA{i}", [65, IB]) for i in range(2)]
            pcc = [ptA(f"pcc{i}", [65, 4]) for i in range(2)]
            prb = [ptA(f"prb{i}", [64, IB]) for i in range(2)]

            def _s(sync):
                for kc in range(KC):
                    qS3.dma(sync.dma_start(out=ktR[:, kc, :], in_=sKT[kc, :, :]))
                for kc in range(KC):
                    qS3.dma(sync.dma_start(out=qtI[0][:, kc, :],
                                           in_=sQT[kc, :, 0:IB]))
            em.sync(_s)
            kt_ld = qS3.n

            for i in range(NI):
                if i + 1 < NI:
                    def _s(sync, i=i, w=M.get(("ablk", i - 1), 0)):
                        sync.wait_ge(qVE.sem, w)
                        for kc in range(KC):
                            qS3.dma(sync.dma_start(
                                out=qtI[(i + 1) % 2][:, kc, :],
                                in_=sQT[kc, :, IB * (i + 1):IB * (i + 2)]))
                    em.sync(_s)
                for g2 in range(2):
                    for hh in range(8):
                        h = 8 * g2 + hh
                        hc, hr = h // 2, (h % 2) * 64
                        jmax = 4 * i + 4
                        numq = {}

                        def emit_num(j):
                            sc_n, ve_m, off = numq.pop(j)

                            def _t(t, h=h, j=j, off=off, sc_n=sc_n, ve_m=ve_m,
                                   w=M.get(("pnA", h % 2), 0)):
                                if ve_m is not None:
                                    t.wait_ge(qVE.sem, ve_m)
                                else:
                                    t.wait_ge(qSC.sem, sc_n)
                                if j == 0:
                                    t.wait_ge(qSC.sem, w)
                                qTE.inc(t.matmul(pnA[h % 2][:, off:IB],
                                                 vAug[:, j, h, :],
                                                 sqb[j % 3][:, off:IB],
                                                 start=(j == 0), stop=False))
                            em.tensor(_t)
                            M[("sqb", j % 3)] = qTE.n

                        for j in range(jmax):
                            m = j - 4 * i
                            bs = j % 2
                            off = 128 * m if m >= 0 else 0

                            def _t(t, i=i, hc=hc, hr=hr, j=j, bs=bs, off=off,
                                   w=M.get(("psA", bs), 0), kt_ld=kt_ld):
                                t.wait_ge(qS3.sem, kt_ld)
                                t.wait_ge(qSC.sem, w)
                                qTE.inc(t.matmul(
                                    psA[bs][:, off:IB],
                                    ktR[hr:hr + 64, hc, 128 * j:128 * (j + 1)],
                                    qtI[i % 2][hr:hr + 64, hc, off:IB],
                                    start=True, stop=True))
                            em.tensor(_t)
                            te_s = qTE.n

                            def _c(c, bs=bs, j=j, off=off, te_s=te_s,
                                   w=M.get(("sqb", j % 3), 0)):
                                c.wait_ge(qTE.sem, te_s)
                                c.wait_ge(qTE.sem, w)
                                qSC.inc(c.activation(out=sqb[j % 3][:, off:IB],
                                                     in_=psA[bs][:, off:IB],
                                                     func=AF.Square, bias=1.0))
                            em.scalar(_c)
                            M[("psA", bs)] = qSC.n
                            sc_n = qSC.n
                            ve_m = None
                            if m >= 0:
                                def _v(v, j=j, m=m, sc_n=sc_n):
                                    v.wait_ge(qSC.sem, sc_n)
                                    qVE.inc(v.tensor_mul(
                                        out=sqb[j % 3][:, 128 * m:128 * (m + 1)],
                                        in0=sqb[j % 3][:, 128 * m:128 * (m + 1)],
                                        in1=cU[:]))
                                em.vector(_v)
                                ve_m = qVE.n
                            numq[j] = (sc_n, ve_m, off)
                            if j >= 1:
                                emit_num(j - 1)
                        emit_num(jmax - 1)

                        def _t(t, i=i, h=h, hh=hh, w=M.get(("pcc", hh % 2), 0)):
                            for m in range(4):
                                t.matmul(pnA[h % 2][:],
                                         vAug[:, 4 * i + m, h, :],
                                         cG[:, 384 - 128 * m:896 - 128 * m],
                                         start=False, stop=(m == 3))
                            t.wait_ge(qVE.sem, w)
                            for m in range(4):
                                inst = t.matmul(pcc[hh % 2][:, m:m + 1],
                                                vAug[:, 4 * i + m, h, :],
                                                cU[:, 127:128], start=True,
                                                stop=True)
                            qTE.inc(inst)
                        em.tensor(_t)
                        te_cum = qTE.n

                        def _c(c, h=h, hh=hh, g2=g2, te_cum=te_cum,
                               w=M.get(("nt", h + 8 if g2 == 0 else h - 8), 0)):
                            c.wait_ge(qTE.sem, te_cum)
                            c.wait_ge(qVE.sem, w)
                            qSC.inc(c.activation(out=nt[:, hh, :], in_=pnA[h % 2][:],
                                                 func=AF.Identity,
                                                 bias=cumA[:, h:h + 1], scale=1.0))
                        em.scalar(_c)
                        M[("pnA", h % 2)] = qSC.n
                        sc_ev = qSC.n

                        def _v(v, h=h, hh=hh, te_cum=te_cum, sc_ev=sc_ev):
                            v.wait_ge(qTE.sem, te_cum)
                            v.wait_ge(qSC.sem, sc_ev)
                            v.tensor_reduce(out=cumT[:], in_=pcc[hh % 2][:],
                                            axis=AX, op=ALU.add)
                            v.tensor_copy(out=filB[:, 0:1], in_=filB[:, 1:2])
                            qVE.inc(v.tensor_add(out=cumA[:, h:h + 1],
                                                 in0=cumA[:, h:h + 1], in1=cumT[:]))
                        em.vector(_v)
                        M[("pcc", hh % 2)] = qVE.n

                        def _g(g, hh=hh, sc_ev=sc_ev):
                            g.wait_ge(qSC.sem, sc_ev)
                            qGD.dma(g.dma_start(out=den[hh:hh + 1, :],
                                                in_=nt[64:65, hh, :]))
                        em.gpsimd(_g)
                    gd_n = qGD.n

                    def _v(v, gd_n=gd_n):
                        v.wait_ge(qGD.sem, gd_n)
                        qVE.inc(v.reciprocal(out=rd[:], in_=den[:]))
                    em.vector(_v)
                    ve_rd = qVE.n
                    for hh in range(8):
                        h = 8 * g2 + hh
                        hc, hr = h // 2, (h % 2) * 64

                        def _t(t, hh=hh, ve_rd=ve_rd, w=M.get(("prb", hh % 2), 0)):
                            t.wait_ge(qVE.sem, ve_rd)
                            t.wait_ge(qVE.sem, w)
                            qTE.inc(t.matmul(prb[hh % 2][:],
                                             cSel[:, 64 * hh:64 * (hh + 1)],
                                             rd[:], start=True, stop=True))
                        em.tensor(_t)
                        te_rb = qTE.n

                        def _v(v, i=i, h=h, hh=hh, hc=hc, hr=hr, te_rb=te_rb,
                               w=M.get(("otst", i - 1), 0), w2=M.get("dvT", 0)):
                            v.wait_ge(qTE.sem, te_rb)
                            v.wait_ge(qS3.sem, w)
                            if hr != 0:
                                v.wait_ge(qGD.sem, w2)
                            dst = otB[0:64, hc, :] if hr == 0 else dvT[:]
                            qVE.inc(v.tensor_tensor(out=dst, in0=nt[0:64, hh, :],
                                                    in1=prb[hh % 2][:],
                                                    op=ALU.mult))
                        em.vector(_v)
                        M[("prb", hh % 2)] = qVE.n
                        M[("nt", h)] = qVE.n
                        ve_dv = qVE.n
                        if hr != 0:
                            def _g(g, hc=hc, ve_dv=ve_dv):
                                g.wait_ge(qVE.sem, ve_dv)
                                qGD.dma(g.dma_start(out=otB[64:128, hc, :],
                                                    in_=dvT[:]))
                            em.gpsimd(_g)
                            M["dvT"] = qGD.n
                M[("ablk", i)] = qVE.n
                ve_blk, gd_blk = qVE.n, qGD.n

                def _s(sync, i=i, ve_blk=ve_blk, gd_blk=gd_blk):
                    sync.wait_ge(qVE.sem, ve_blk)
                    sync.wait_ge(qGD.sem, gd_blk)
                    for kc in range(KC):
                        qS3.dma(sync.dma_start(out=sOT[kc, :, IB * i:IB * (i + 1)],
                                               in_=otB[:, kc, :]))
                em.sync(_s)
                M[("otst", i)] = qS3.n
            barrier(em, qS3)
            esA.close()
            esV.close()

        if K >= 5:
            em.set_phase("f")
            esN = ExitStack()
            sbN = lambda n, s, d=F32: esN.enter_context(nc.sbuf_tensor(n, s, d))
            ptN = lambda n, s: esN.enter_context(nc.psum_tensor(n, s, F32))
            xqTb = sbN("xqTb", [128, KC, MW], BF16)
            gwc = [sbN(f"gwc{i}", [128, KC, ICW], BF16) for i in range(2)]
            uwc = [sbN(f"uwc{i}", [128, KC, ICW], BF16) for i in range(2)]
            dwc = [sbN(f"dwc{i}", [128, 4, D], BF16) for i in range(3)]
            woTb = sbN("woTb", [128, KC, D], BF16)
            accB = sbN("accB", [128, 8, D])
            hs = sbN("hs", [128, ICW])
            hbuf = sbN("hbuf", [128, ICW])
            qbN = quant_bufs("n", ICW, sbN)
            fdrn = sbN("fdrn", [128, 128])
            ax2 = [qbN["ax"], sbN("nax1", [128, ICW])]
            sg2 = [qbN["sg"], sbN("nsg1", [128, ICW])]
            hs2 = [sbN("nhs0", [128, ICW]), sbN("nhs1", [128, ICW])]
            hb2 = [sbN("nhb0", [128, ICW]), sbN("nhb1", [128, ICW])]
            hq = [sbN(f"hq{i}", [128, ICW], BF16) for i in range(2)]
            hqT = [sbN(f"hqT{i}", [128, 4, 128], BF16) for i in range(2)]
            xfin = [sbN(f"xfin{i}", [128, D]) for i in range(2)]
            otT = [sbN(f"otT{i}", [128, KC, 128], BF16) for i in range(2)]
            gbuf = sbN("gbuf", [128, D])
            outb = [sbN(f"outb{i}", [128, D]) for i in range(2)]
            pgu = [ptN(f"pgu{i}", [128, ICW]) for i in range(4)]
            pd0 = ptN("pd0", [128, D])
            pd = [pd0, pd0]
            psS0 = ptN("psS0", [128, D])
            psS = [psS0, psS0]

            def _s(sync):
                for kc in range(KC):
                    qS4.dma(sync.dma_start(out=woTb[:, kc, :], in_=sWO[kc, :, :]))
            em.sync(_s)
            wo_ld = qS4.n

            def _v(v):
                qVE.inc(v.memset(fdrn[:], 0.0))
            em.vector(_v)

            def emit_down(n):
                s = n % 2
                ws = (n // 8) % 3

                def _t(t, s=s, ws=ws, w=M[("hqTsy", n)], w2=M.get(("acc", n - 1), 0)):
                    t.wait_ge(qS4.sem, w)
                    t.wait_ge(qVE.sem, w2)
                    for j in range(4):
                        for nh in range(2):
                            inst = t.matmul(
                                pd0[:, 512 * nh:512 * (nh + 1)],
                                hqT[s][:, j, :],
                                dwc[ws][:, j, 512 * nh:512 * (nh + 1)],
                                start=(j == 0), stop=(j == 3))
                    qTE.inc(inst)
                em.tensor(_t)
                M[("dn", n)] = qTE.n

            def emit_acc(n):
                ic = (n // 8) % NIC
                tt = n % 8

                def _v(v, ic=ic, tt=tt, w=M[("dn", n)],
                       w2=M.get(("finsilu", tt), 0)):
                    v.wait_ge(qTE.sem, w)
                    if ic == 0:
                        v.wait_ge(qSC.sem, w2)
                        v.tensor_scalar_mul(out=accB[:, tt, :],
                                            in0=pd0[:],
                                            scalar1=cSc[:, 2:3])
                    else:
                        v.scalar_tensor_tensor(
                            out=accB[:, tt, :], in0=pd0[:],
                            scalar=cSc[:, 2:3], in1=accB[:, tt, :],
                            op0=ALU.mult, op1=ALU.add)
                    v.tensor_copy(out=fdrn[:, 0:64], in_=fdrn[:, 64:128])
                    qVE.inc(v.tensor_copy(out=fdrn[:, 64:128], in_=fdrn[:, 0:64]))
                em.vector(_v)
                M[("acc", n)] = qVE.n
                if ic == NIC - 1:
                    M[("accdone", tt)] = qVE.n

            def emit_finals(mb):
                for tt in range(8):
                    emit_final_tile(mb, tt)

            def emit_final_tile(mb, tt):
                if True:
                    n = 8 * mb + tt
                    r = tt % 2

                    def _s(sync, n=n, r=r, w=M.get(("fin", r), 0)):
                        sync.wait_ge(qVE.sem, w)
                        qS4.dma(sync.dma_start(out=xfin[r][:],
                                               in_=x_h[128 * n:128 * (n + 1), :]))
                        for kc in range(KC):
                            qS4.dma(sync.dma_start(
                                out=otT[r][:, kc, :],
                                in_=sOT[kc, :, 128 * n:128 * (n + 1)]))
                    em.sync(_s)
                    fin_ld = qS4.n

                    def _c(c, tt=tt, w=M.get("gbuf", 0),
                           w2=M[("accdone", tt)]):
                        c.wait_ge(qVE.sem, w)
                        c.wait_ge(qVE.sem, w2)
                        qSC.inc(c.activation(out=gbuf[:], in_=accB[:, tt, :],
                                             func=AF.Silu))
                    em.scalar(_c)
                    M[("finsilu", tt)] = qSC.n
                    sc_g = qSC.n

                    def _t(t, r=r, fin_ld=fin_ld, w=M.get("psSx", 0),
                           wo_ld=wo_ld):
                        t.wait_ge(qS4.sem, max(fin_ld, wo_ld))
                        t.wait_ge(qVE.sem, w)
                        for kc in range(KC):
                            for nh in range(2):
                                inst = t.matmul(psS[r][:, 512 * nh:512 * (nh + 1)],
                                                otT[r][:, kc, :],
                                                woTb[:, kc, 512 * nh:512 * (nh + 1)],
                                                start=(kc == 0), stop=(kc == KC - 1))
                        qTE.inc(inst)
                    em.tensor(_t)
                    te_s = qTE.n

                    def _v(v, n=n, r=r, te_s=te_s, sc_g=sc_g,
                           w=M.get(("outb", r), 0)):
                        v.wait_ge(qTE.sem, te_s)
                        v.wait_ge(qSC.sem, sc_g)
                        v.wait_ge(qS4.sem, w)
                        v.tensor_tensor(out=gbuf[:], in0=gbuf[:], in1=psS[r][:],
                                        op=ALU.mult)
                        qVE.inc(v.tensor_add(out=outb[r][:], in0=xfin[r][:],
                                             in1=gbuf[:]))
                    em.vector(_v)
                    M["psSx"] = qVE.n
                    M[("fin", r)] = qVE.n
                    M["gbuf"] = qVE.n
                    ve_out = qVE.n

                    def _s(sync, n=n, r=r, ve_out=ve_out):
                        sync.wait_ge(qVE.sem, ve_out)
                        qS4.dma(sync.dma_start(out=y_h[128 * n:128 * (n + 1), :],
                                               in_=outb[r][:]))
                    em.sync(_s)
                    M[("outb", r)] = qS4.n

            _gch = int(_os.environ.get("GRPCHUNK", "4"))
            for n in range(NMAC * NIC * 8):
                mb, ic, tt = n // 64, (n // 8) % NIC, n % 8
                s = n % 2
                gidx = n // 8
                ws = gidx % 2
                if _gch and n % _gch == 0:
                    em.flush()
                if ic == 0:
                    def _s(sync, mb=mb, tt=tt,
                           w=M.get(("gu", 64 * mb - 8 + tt), 0)):
                        sync.wait_ge(qTE.sem, w)
                        for kc in range(KC):
                            qS4.dma(sync.dma_start(
                                out=xqTb[:, kc, 128 * tt:128 * (tt + 1)],
                                in_=sXQ[8 * mb + tt, kc]))
                    em.sync(_s)
                    M["xqld"] = qS4.n
                if tt == 0:

                    if gidx == 0:
                        def _s(sync):
                            for kc in range(KC):
                                for j in range(4):
                                    qS4.dma(sync.dma_start(
                                        out=gwc[0][:, kc, 128 * j:128 * (j + 1)],
                                        in_=sWG[j, kc]))
                                    qS4.dma(sync.dma_start(
                                        out=uwc[0][:, kc, 128 * j:128 * (j + 1)],
                                        in_=sWU[j, kc]))
                            for j in range(4):
                                for rr in range(8):
                                    qS4.dma(sync.dma_start(
                                        out=dwc[0][:, j, 128 * rr:128 * (rr + 1)],
                                        in_=sWD[rr, j]))
                        em.sync(_s)
                        M[("wcld", 0)] = qS4.n

                gld = gidx + 1
                if gld < NMAC * NIC:
                    wsg, wsd = gld % 2, gld % 3
                    icl = gld % NIC

                    def _s(sync, tt=tt, wsg=wsg, wsd=wsd, icl=icl,
                           wg=M.get(("gu", 8 * (gld - 2) + 7), 0),
                           wd=M.get(("dn", 8 * (gld - 3) + 7), 0)):
                        sync.wait_ge(qTE.sem, max(wg, wd))
                        for j in range(4):
                            qS4.dma(sync.dma_start(
                                out=gwc[wsg][:, tt, 128 * j:128 * (j + 1)],
                                in_=sWG[4 * icl + j, tt]))
                            qS4.dma(sync.dma_start(
                                out=uwc[wsg][:, tt, 128 * j:128 * (j + 1)],
                                in_=sWU[4 * icl + j, tt]))
                        j = tt // 2
                        for rr in range(4 * (tt % 2), 4 * (tt % 2) + 4):
                            qS4.dma(sync.dma_start(
                                out=dwc[wsd][:, j, 128 * rr:128 * (rr + 1)],
                                in_=sWD[rr, 4 * icl + j]))
                    em.sync(_s)
                    M[("wcld", gld)] = qS4.n

                def _t(t, tt=tt, s=s, ws=ws, gd_w=max(M["xqld"], M[("wcld", gidx)]),
                       w=M.get(("silu", n - 2), 0), w2=M.get(("hbuf", n - 2), 0)):
                    t.wait_ge(qS4.sem, gd_w)
                    t.wait_ge(qSC.sem, w)
                    t.wait_ge(qVE.sem, w2)
                    for kc in range(KC):
                        t.matmul(pgu[2 * s][:],
                                 xqTb[:, kc, 128 * tt:128 * (tt + 1)],
                                 gwc[ws][:, kc, :],
                                 start=(kc == 0), stop=(kc == KC - 1))
                    for kc in range(KC):
                        inst = t.matmul(pgu[2 * s + 1][:],
                                        xqTb[:, kc, 128 * tt:128 * (tt + 1)],
                                        uwc[ws][:, kc, :],
                                        start=(kc == 0), stop=(kc == KC - 1))
                    qTE.inc(inst)
                em.tensor(_t)
                M[("gu", n)] = qTE.n
                te_gu = qTE.n
                if n >= 1:
                    emit_down(n - 1)

                def _c(c, s=s, te_gu=te_gu, w=M.get(("hbuf", n - 2), 0)):
                    c.wait_ge(qTE.sem, te_gu)
                    c.wait_ge(qVE.sem, w)
                    c.activation(out=hs2[s][:], in_=pgu[2 * s][:],
                                 func=AF.Silu, scale=cSc[:, 0:1])
                    qSC.inc(c.activation(out=fdrn[:, 0:64], in_=fdrn[:, 64:128],
                                         func=AF.Square))
                em.scalar(_c)
                M[("silu", n)] = qSC.n
                sc_si = qSC.n

                def _v(v, s=s, sc_si=sc_si, w=M.get(("absg", n - 2), 0)):
                    v.wait_ge(qSC.sem, sc_si)
                    v.wait_ge(qSC.sem, w)
                    v.scalar_tensor_tensor(
                        out=hb2[s][:], in0=pgu[2 * s + 1][:], scalar=cSc[:, 1:2],
                        in1=hs2[s][:], op0=ALU.mult, op1=ALU.mult)
                    v.tensor_copy(out=fdrn[:, 0:64], in_=fdrn[:, 64:128])
                    qVE.inc(v.tensor_copy(out=fdrn[:, 64:128], in_=fdrn[:, 0:64]))
                em.vector(_v)
                M[("hbuf", n)] = qVE.n
                ve_h = qVE.n

                def _c(c, s=s, ve_h=ve_h, w=M.get(("hquant", n - 2), 0)):
                    c.wait_ge(qVE.sem, ve_h)
                    c.wait_ge(qVE.sem, w)
                    c.activation(out=ax2[s][:], in_=hb2[s][:], func=AF.Abs)
                    c.sign(out=sg2[s][:], in_=hb2[s][:])
                    qSC.inc(c.activation(out=fdrn[:, 0:64], in_=fdrn[:, 64:128],
                                         func=AF.Square))
                em.scalar(_c)
                M[("absg", n)] = qSC.n
                sc_ab = qSC.n

                def _v(v, s=s, sc_ab=sc_ab, w=M.get(("hqTsy", n - 2), 0)):
                    v.wait_ge(qSC.sem, sc_ab)
                    v.wait_ge(qS4.sem, w)
                    qbN["ax"], qbN["sg"] = ax2[s], sg2[s]
                    qVE.inc(emit_quant(v, hq[s][:], ICW, sind6[:],
                                       rsind[:], qbN))
                em.vector(_v)
                M[("hquant", n)] = qVE.n
                ve_hq = qVE.n
                if n >= 1:
                    emit_acc(n - 1)
                if n >= 64 and n % 64 < 8:
                    emit_final_tile(mb - 1, n % 64)

                def _s(sync, s=s, ve_hq=ve_hq, w=M.get(("dn", n - 2), 0)):
                    sync.wait_ge(qVE.sem, ve_hq)
                    sync.wait_ge(qTE.sem, w)
                    sync.wait_ge(qS4.sem, qS4.n)
                    for j in range(4):
                        qS4.dma(sync.dma_start(
                            out=hqT[s][:, j, :],
                            in_=hq[s][:, 128 * j:128 * (j + 1)],
                            transpose=True))
                    sync.wait_ge(qS4.sem, qS4.n)
                em.sync(_s)
                M[("hqTsy", n)] = qS4.n

            NLAST = NMAC * NIC * 8 - 1
            em.flush()
            emit_down(NLAST)
            emit_acc(NLAST)
            em.flush()
            emit_finals(NMAC - 1)
            em.flush()
            fin_sy = qS4.n

            def _s(sync, fin_sy=fin_sy):
                sync.wait_ge(qS4.sem, fin_sy)
                sync.nop()
            em.sync(_s)
            em.flush()
            if tap == "f1":
                emit_tap(em, [("accB", accB[:]), ("gbuf", gbuf[:]),
                                 ("hq", hq[0][:]), ("hs", hs2[0][:]),
                                 ("hbuf", hb2[0][:]), ("xqTb", xqTb[:]),
                                 ("gwc0", gwc[0][:]), ("dwc0", dwc[0][:]),
                                 ("woTb", woTb[:]), ("otT0", otT[0][:]),
                                 ("nax", qbN["ax"][:]), ("nam", qbN["am"][:]),
                                 ("ns8", qbN["s8"][:])], qS4)
            esN.close()
        if K == 0:
            emit_tap(em, [("sin6", sin6[:]), ("rsin", rsin[:]),
                             ("sind6", sind6[:]), ("rsind", rsind[:]),
                             ("cU", cU[:])], qS0)
        elif K == 1:
            emit_tap(em, [("wTq", wT["wq"][:]), ("wTk", wT["wk"][:]),
                             ("wTv", wT["wv"][:]),
                             ("sWO", sWO, [KC, 128, D], BF16)], qS0)
        elif K == 4:
            emit_tap(em, [("sOT", sOT, [KC, 128, T], BF16),
                             ("cumA", cumA[:])], qS3)
        em.flush()
        if K == 1:
            esWT.close()
        if K < 4:
            esV.close()
    es.close()
    return nc


def kernel(L0_attn_x, L0_attn_Wq, L0_attn_Wk, L0_attn_Wv, L0_attn_Wo,
           L0_ffn_gate_w, L0_ffn_up_w, L0_ffn_down_w, L0_ffn_s_in,
           L0_ffn_s_in_down):
    x = np.ascontiguousarray(np.asarray(L0_attn_x, dtype=np.float32))
    B = x.shape[0]
    consts = _make_consts()
    nc = _build_program()
    shared = dict(
        wq=np.ascontiguousarray(L0_attn_Wq, np.float32),
        wk=np.ascontiguousarray(L0_attn_Wk, np.float32),
        wv=np.ascontiguousarray(L0_attn_Wv, np.float32),
        wo=np.ascontiguousarray(L0_attn_Wo, np.float32),
        gw=np.ascontiguousarray(L0_ffn_gate_w, np.float32),
        uw=np.ascontiguousarray(L0_ffn_up_w, np.float32),
        dw=np.ascontiguousarray(L0_ffn_down_w, np.float32),
        sin=np.asarray(L0_ffn_s_in, np.float32).reshape(1, 1),
        sind=np.asarray(L0_ffn_s_in_down, np.float32).reshape(1, 1),
        **consts,
    )
    in_maps = [dict(x=np.ascontiguousarray(x[b]), **shared) for b in range(B)]
    res = run_bass_kernel_spmd(nc, in_maps, core_ids=list(range(B)))
    return np.stack([np.asarray(r["y"], dtype=np.float32)
                     for r in res.results], 0)
